# version 24
# baseline (speedup 1.0000x reference)
import numpy as np
import concourse.bass as bass
import concourse.mybir as mybir
from concourse.bass_utils import run_bass_kernel_spmd
from contextlib import ExitStack

F32 = mybir.dt.float32
BF16 = mybir.dt.bfloat16
OP = mybir.AluOpType
AF = mybir.ActivationFunctionType
AX = mybir.AxisListType

P = 128
D = 1024
DFF = 4096
NH, DH = 16, 64
NB = 2
TPB = 32
NT = NB * TPB
KD = D // P
KF = DFF // P
CM = 12582912.0
R1024 = 1.0 / 1024.0
DBG_B = None
RMEAN_W = 1.0 / (DFF * D)

AF_COLS = 16384
PH = 0
PERSIST = 11264
QK_ALL = PERSIST
KSUM_BC = PERSIST + 1024
GB_BC = PERSIST + 3072
AB_COLS = 65536
XT_H = 57344
XT_L = 59392
ZG_H = 61440
ZG_L = 62464


def build(stage=3, dbg=None):
    nc = bass.Bass("TRN2", target_bir_lowering=False)
    ap = lambda t: t.ap()
    x_h = ap(nc.dram_tensor("L0_attn_hidden_states", [NT * P, D], F32, kind="ExternalInput"))
    wq_h = ap(nc.dram_tensor("L0_attn_Wq", [D, D], F32, kind="ExternalInput"))
    wk_h = ap(nc.dram_tensor("L0_attn_Wk", [D, D], F32, kind="ExternalInput"))
    wv_h = ap(nc.dram_tensor("L0_attn_Wv", [D, D], F32, kind="ExternalInput"))
    wg_h = ap(nc.dram_tensor("L0_attn_Wg", [D, D], F32, kind="ExternalInput"))
    wo_h = ap(nc.dram_tensor("L0_attn_Wo", [D, D], F32, kind="ExternalInput"))
    gam_h = ap(nc.dram_tensor("L0_attn_gamma", [1, D], F32, kind="ExternalInput"))
    bet_h = ap(nc.dram_tensor("L0_attn_beta", [1, D], F32, kind="ExternalInput"))
    wgate_h = ap(nc.dram_tensor("L0_ffn_w_gate", [DFF, D], F32, kind="ExternalInput"))
    wup_h = ap(nc.dram_tensor("L0_ffn_w_up", [DFF, D], F32, kind="ExternalInput"))
    wdown_h = ap(nc.dram_tensor("L0_ffn_w_down", [D, DFF], F32, kind="ExternalInput"))
    out_h = ap(nc.dram_tensor("out", [NT * P, D], F32, kind="ExternalOutput"))
    q_scr = ap(nc.dram_tensor("q_scr", [NT * P, D], F32, kind="Internal"))
    y_scr = ap(nc.dram_tensor("y_scr", [NT * P, D], F32, kind="Internal"))
    ih_scr = ap(nc.dram_tensor("ih_scr", [NT * P, DFF], BF16, kind="Internal"))
    tg_scr = ap(nc.dram_tensor("tg_scr", [DFF, D], BF16, kind="Internal"))
    tu_scr = ap(nc.dram_tensor("tu_scr", [DFF, D], BF16, kind="Internal"))
    td_scr = ap(nc.dram_tensor("td_scr", [D, DFF], BF16, kind="Internal"))
    dbg_h = ap(nc.dram_tensor("dbg", [NT * P, 9344], F32, kind="ExternalOutput")) if dbg else None
    dbg2_h = ap(nc.dram_tensor("dbg2", [P, AB_COLS], BF16, kind="ExternalOutput")) if dbg else None
    ffn_w_h = [wgate_h, wup_h, wdown_h]
    ffn_scr = [tg_scr, tu_scr, td_scr]

    with ExitStack() as ctx:
        e = ctx.enter_context
        af_t = e(nc.sbuf_tensor("af", [P, AF_COLS], F32))
        ab_t = e(nc.sbuf_tensor("ab", [P, AB_COLS], BF16))
        af = af_t[:]
        ab = ab_t[:]
        ident = e(nc.sbuf_tensor("ident", [P, P], F32))
        ones = e(nc.sbuf_tensor("ones", [P, 1], F32))
        onescol = e(nc.sbuf_tensor("onescol", [1, P], F32))
        racc = e(nc.sbuf_tensor("racc", [1, 8], F32))
        racc128 = e(nc.sbuf_tensor("racc128", [P, 4], F32))
        swu = e(nc.sbuf_tensor("swu", [1, 8], F32))
        swbc = e(nc.sbuf_tensor("swbc", [P, 4], F32))
        uwbc = e(nc.sbuf_tensor("uwbc", [P, 4], F32))
        wfr = e(nc.sbuf_tensor("wfr", [P, 2048], F32))
        tfr = e(nc.sbuf_tensor("tfr", [P, 2048], BF16))
        cd_all = e(nc.sbuf_tensor("cd_all", [P, NT], F32))
        cmb = e(nc.sbuf_tensor("cmb", [P, 1], F32))
        tv = e(nc.sbuf_tensor("tv", [P, 96], F32))
        MX, MC, RECM, SXV, CGc, CUc = 0, 1, 2, 3, 4, 5
        ZSUM, MUz, ZSS, VVz, SVz, R0z, RRz, AAz, BBz, RSz = 6, 7, 8, 9, 10, 11, 12, 13, 14, 15
        YSUM, MUy, YSS, VVy, SVy, R0y, RRy, AAy, BBy, RSy = 16, 17, 18, 19, 20, 21, 22, 23, 24, 25
        HM8 = 32
        HM, MH, RH, SH, REC3 = 40, 41, 42, 43, 44
        NRM16 = 48
        REC16 = 64
        R16 = 80
        ps_a = e(nc.psum_tensor("psA", [P, 1024], F32))
        ps_b = e(nc.psum_tensor("psB", [P, 1024], F32))
        ps_c = e(nc.psum_tensor("psC", [P, 1024], F32))
        ps_d = e(nc.psum_tensor("psD", [P, 1024], F32))
        pA, pB, pC, pD = ps_a[:], ps_b[:], ps_c[:], ps_d[:]

        sems = {}
        for name in [
            "BAR", "sG_init", "sP_w", "sP_wsplit", "sA_x", "sA_tr", "sA_split",
            "sA_evac", "sA_qk", "sA_minq", "sA_mink", "sA_expq", "sA_expk",
            "sA_q1", "sA_k1", "sA_ksmm", "sA_kscopy", "sA_ksbc", "sA_ksbccopy",
            "st_q", "sF_dma", "sF_gm", "sF_sum", "sF_scal", "sF_bc", "sF_qdma",
            "st_tern", "s_gb", "sF_vr", "sB_w", "sB_wsplit", "sB_x", "sB_q", "sB_tr",
            "sB_split", "sB_P", "sB_vg", "sB_z", "sB_sqz", "sB_varz", "sB_sqrtz",
            "sB_sg", "sB_zg", "sB_zgtr", "sB_zgsplit", "sB_wo", "sB_y", "sB_sqy",
            "sB_vary", "sB_sqrty", "sB_y1", "st_y", "sC_tern", "sC_y", "sC_intx",
            "sC_ixtr", "sC_ixT", "sC_gu", "sC_sg", "sC_h", "sC_inth", "sC_ihtr",
            "sC_ihT", "st_ih", "sD_tern", "sD_ih", "sD_y", "sD_mm", "sD_y2",
            "sD_sq", "sD_var", "sD_sqrt", "sD_out", "st_out", "sHZ", "sDbg", "sF_gp", "sF_s1", "sF_vq", "sC_ih1", "sC_sh",
        ]:
            sems[name] = e(nc.semaphore(name))
        S = type("S", (), sems)

        NPHASE = {1: 1, 2: 2, 3: 4}[stage]

        def bar_val(p):
            return 5 * (p + 1)

        def attn_proj_pass(te, tau, projs, pi):
            par = tau % 2
            xh = ab[:, XT_H + par * 1024: XT_H + par * 1024 + 1024]
            xl = ab[:, XT_L + par * 1024: XT_L + par * 1024 + 1024]
            xa = [xh, xh, xl][pi]
            last = None
            for ki in range(KD):
                lhs = xa[:, ki * P:(ki + 1) * P]
                for (ps, hb, lb) in projs:
                    wb = hb if pi != 1 else lb
                    for no in range(2):
                        last = te.matmul(
                            ps[:, no * 512:(no + 1) * 512], lhs,
                            ab[:, wb + ki * 1024 + no * 512: wb + ki * 1024 + (no + 1) * 512],
                            start=(pi == 0 and ki == 0), stop=(pi == 2 and ki == KD - 1))
            return last

        def attn_proj_mms(te, tau, projs):
            for pi in range(3):
                last = attn_proj_pass(te, tau, projs, pi)
            return last

        def transpose8(te, dst_psum, src_f32):
            last = None
            for ki in range(KD):
                last = te.transpose(dst_psum[:, ki * P:(ki + 1) * P],
                                    src_f32[:, ki * P:(ki + 1) * P], ident[:])
            return last

        def newton_rsqrt(t_in, sv, r0, rr, aa, bb, rs):
            vrecip(r0, sv)
            vtt(rr, r0, r0, OP.mult)
            vtt(aa, t_in, rr, OP.mult)
            vts(bb, aa, -0.5, 1.5, OP.mult, OP.add)
            return vtt(rs, r0, bb, OP.mult)

        tvc = lambda c: tv[:, c:c + 1]
        hzc = [0]

        def vrecip(out_ap, in_ap):
            hzc[0] += 1
            nc.vector.reciprocal(out_ap, in_ap).then_inc(S.sHZ, 1)
            nc.vector.wait_ge(S.sHZ, hzc[0])

        def _vhard(i, fence=None):
            if fence is None:
                hzc[0] += 1
                i.then_inc(S.sHZ, 1)
                nc.vector.wait_ge(S.sHZ, hzc[0])
            else:
                sem, val = fence
                i.then_inc(sem, 1)
                nc.vector.wait_ge(sem, val)
            return i

        def vts(out_ap, in0, s1, s2, op0, op1=None, fence=None):
            i = (nc.vector.tensor_scalar(out_ap, in0, s1, s2, op0, op1)
                 if op1 is not None else
                 nc.vector.tensor_scalar(out_ap, in0, s1, None, op0))
            return _vhard(i, fence)

        def vtt(out_ap, a_ap, b_ap, op, fence=None):
            i = nc.vector.tensor_tensor(out_ap, a_ap, b_ap, op)
            return _vhard(i, fence)

        def vred(out_ap, in_ap, ax, op, absval=None, fence=None):
            i = nc.vector.tensor_reduce(out_ap, in_ap, ax, op,
                                        apply_absolute_value=absval)
            return _vhard(i, fence)


        A_WQ_H, A_WQ_L, A_WK_H, A_WK_L = 0, 8192, 16384, 24576
        B_WV_H, B_WV_L, B_WG_H, B_WG_L, B_WO_H, B_WO_L = 0, 8192, 16384, 24576, 32768, 40960

        def afs(c0, n=1024):
            return af[:, c0:c0 + n]

        def wslot(i):
            k = i % 3
            if k < 2:
                return wfr[:, k * 1024:(k + 1) * 1024]
            return af[:, PH + 10240:PH + 11264]

        ffn_ntile = 96

        def ffn_tile_src(i):
            m, s = i // 32, i % 32
            if m < 2:
                return ffn_w_h[m][s * P:(s + 1) * P, :]
            r, q = s // 4, s % 4
            return wdown_h[r * P:(r + 1) * P, q * 1024:(q + 1) * 1024]

        def ffn_tile_dst(i):
            m, s = i // 32, i % 32
            if m < 2:
                return ffn_scr[m][s * P:(s + 1) * P, :]
            r, q = s // 4, s % 4
            return td_scr[r * P:(r + 1) * P, q * 1024:(q + 1) * 1024]

        block = e(nc.Block())

        @block.sync
        def _(sy):
            sy.dma_start(out=af[0:1, GB_BC:GB_BC + 1024], in_=gam_h[0:1, :]).then_inc(S.s_gb, 16)
            sy.dma_start(out=af[0:1, GB_BC + 1024:GB_BC + 2048], in_=bet_h[0:1, :]).then_inc(S.s_gb, 16)
            for i in range(16):
                wsrc = wq_h if i < 8 else wk_h
                ci = i % 8
                if i >= 2:
                    sy.wait_ge(S.sP_wsplit, i - 1)
                sy.dma_start(out=afs(PH + (i % 2) * 1024),
                             in_=wsrc[ci * P:(ci + 1) * P, :]).then_inc(S.sP_w, 16)
            for t in range(NT):
                if t >= 2:
                    sy.wait_ge(S.sA_tr, t - 1)
                else:
                    sy.wait_ge(S.sP_wsplit, 15 + t)
                sy.dma_start(out=afs(PH + (t % 2) * 1024),
                             in_=x_h[t * P:(t + 1) * P, :]).then_inc(S.sA_x, 16)
            sy.sem_inc(S.BAR, 1)
            sy.wait_ge(S.BAR, bar_val(0))
            if NPHASE >= 2:
                for i in range(24):
                    wsrc = [wv_h, wg_h, wo_h][i // 8]
                    ci = i % 8
                    if i >= 2:
                        sy.wait_ge(S.sB_wsplit, i - 1)
                    sy.dma_start(out=afs(PH + (i % 2) * 1024),
                                 in_=wsrc[ci * P:(ci + 1) * P, :]).then_inc(S.sB_w, 16)
                for t in range(NT):
                    if t >= 3:
                        sy.wait_ge(S.sB_y, t - 2)
                    elif t < 2:
                        sy.wait_ge(S.sB_wsplit, 23 + t)
                    sy.dma_start(out=afs(PH + (t % 3) * 1024),
                                 in_=x_h[t * P:(t + 1) * P, :]).then_inc(S.sB_x, 16)
                    if t >= 2:
                        sy.wait_ge(S.sB_P, t - 1)
                        sy.wait_ge(S.sB_zgtr, t - 1)
                    sy.dma_start(out=afs(PH + 3072 + (t % 2) * 1024),
                                 in_=q_scr[t * P:(t + 1) * P, :]).then_inc(S.sB_q, 16)
                if NPHASE >= 4:
                    sy.wait_ge(S.sB_vg, NT)
                    for ki in range(KD):
                        for ob in range(8):
                            sy.dma_start(
                                out=ab[:, ki * 4096 + ob * 512: ki * 4096 + (ob + 1) * 512],
                                in_=tg_scr[ob * 512:(ob + 1) * 512, ki * P:(ki + 1) * P],
                                transpose=True).then_inc(S.sC_tern, 16)
                    sy.wait_ge(S.sB_wo, NT)
                    sy.wait_ge(S.sB_zgsplit, NT)
                    for ki in range(KD):
                        for ob in range(8):
                            sy.dma_start(
                                out=ab[:, 32768 + ki * 4096 + ob * 512: 32768 + ki * 4096 + (ob + 1) * 512],
                                in_=tu_scr[ob * 512:(ob + 1) * 512, ki * P:(ki + 1) * P],
                                transpose=True).then_inc(S.sC_tern, 16)
                    sy.wait_ge(S.sB_y, NT)
                    for t in range(2):
                        sy.dma_start(out=afs(PH + (t % 2) * 1024),
                                     in_=y_scr[t * P:(t + 1) * P, :]).then_inc(S.sC_y, 16)
                sy.sem_inc(S.BAR, 1)
                sy.wait_ge(S.BAR, bar_val(1))
            if NPHASE >= 4:
                for t in range(2, NT):
                    if t >= 2:
                        sy.wait_ge(S.sC_intx, t - 1)
                    sy.dma_start(out=afs(PH + (t % 2) * 1024),
                                 in_=y_scr[t * P:(t + 1) * P, :]).then_inc(S.sC_y, 16)
                sy.wait_ge(S.sC_gu, 8 * NT)
                for ki in range(KF):
                    for no in range(2):
                        sy.dma_start(
                            out=ab[:, ki * 1024 + no * 512: ki * 1024 + (no + 1) * 512],
                            in_=td_scr[no * 512:(no + 1) * 512, ki * P:(ki + 1) * P],
                            transpose=True).then_inc(S.sD_tern, 16)
                sy.sem_inc(S.BAR, 1)
                sy.wait_ge(S.BAR, bar_val(2))
                for t in range(NT):
                    if t >= 2:
                        sy.wait_ge(S.sD_mm, t - 1)
                    sy.dma_start(out=af[:, PH + (t % 2) * 2048: PH + (t % 2) * 2048 + 2048].bitcast(BF16),
                                 in_=ih_scr[t * P:(t + 1) * P, :]).then_inc(S.sD_ih, 16)
                    if t >= 2:
                        sy.wait_ge(S.sD_y2, t - 1)
                    sy.dma_start(out=afs(PH + 4096 + (t % 2) * 1024),
                                 in_=y_scr[t * P:(t + 1) * P, :]).then_inc(S.sD_y, 16)
                sy.sem_inc(S.BAR, 1)
                sy.wait_ge(S.BAR, bar_val(3))

        @block.tensor
        def _(te):
            te.wait_ge(S.sG_init, 1)
            te.wait_ge(S.sP_wsplit, 16)
            te.wait_ge(S.sA_x, 16)
            transpose8(te, pD, afs(PH + 0)).then_inc(S.sA_tr, 1)
            for t in range(NT):
                te.wait_ge(S.sA_split, t + 1)
                te.wait_ge(S.sA_evac, t)
                attn_proj_pass(te, t, [(pA, A_WQ_H, A_WQ_L), (pB, A_WK_H, A_WK_L)], 0)
                attn_proj_pass(te, t, [(pA, A_WQ_H, A_WQ_L), (pB, A_WK_H, A_WK_L)], 1)
                if t + 1 < NT:
                    te.wait_ge(S.sA_x, 16 * (t + 2))
                    transpose8(te, pD, afs(PH + ((t + 1) % 2) * 1024)).then_inc(S.sA_tr, 1)
                if t >= 1:
                    sg = t - 1
                    te.wait_ge(S.sA_k1, sg + 1)
                    if sg == TPB:
                        te.wait_ge(S.sA_ksbccopy, 1)
                    k_sb = afs(PH + 4096 + (sg % 2) * 1024)
                    for no in range(2):
                        mm = te.matmul(pC[0:1, no * 512:(no + 1) * 512], ones[:],
                                       k_sb[:, no * 512:(no + 1) * 512],
                                       start=(sg % TPB == 0), stop=(sg % TPB == TPB - 1))
                    mm.then_inc(S.sA_ksmm, 1)
                    if sg % TPB == TPB - 1:
                        b = sg // TPB
                        te.wait_ge(S.sA_kscopy, b + 1)
                        for no in range(2):
                            mm = te.matmul(pC[:, no * 512:(no + 1) * 512], onescol[:],
                                           af[0:1, KSUM_BC + b * 1024 + no * 512: KSUM_BC + b * 1024 + (no + 1) * 512],
                                           start=True, stop=True)
                        mm.then_inc(S.sA_ksbc, 1)
                last = attn_proj_pass(te, t, [(pA, A_WQ_H, A_WQ_L), (pB, A_WK_H, A_WK_L)], 2)
                last.then_inc(S.sA_qk, 1)
            sg = NT - 1
            te.wait_ge(S.sA_k1, sg + 1)
            k_sb = afs(PH + 4096 + (sg % 2) * 1024)
            for no in range(2):
                mm = te.matmul(pC[0:1, no * 512:(no + 1) * 512], ones[:],
                               k_sb[:, no * 512:(no + 1) * 512],
                               start=(sg % TPB == 0), stop=True)
            mm.then_inc(S.sA_ksmm, 1)
            te.wait_ge(S.sA_kscopy, 2)
            for no in range(2):
                mm = te.matmul(pC[:, no * 512:(no + 1) * 512], onescol[:],
                               af[0:1, KSUM_BC + 1024 + no * 512: KSUM_BC + 1024 + (no + 1) * 512],
                               start=True, stop=True)
            mm.then_inc(S.sA_ksbc, 1)
            te.sem_inc(S.BAR, 1)
            te.wait_ge(S.BAR, bar_val(0))
            if NPHASE >= 2:
                te.wait_ge(S.sB_wsplit, 24)
                te.wait_ge(S.sB_x, 16)
                transpose8(te, pD, afs(PH + 0)).then_inc(S.sB_tr, 1)
                for t in range(NT + 1):
                    if t < NT:
                        te.wait_ge(S.sB_split, t + 1)
                        if t >= 1:
                            te.wait_ge(S.sB_z, t)
                            te.wait_ge(S.sB_sg, t)
                        attn_proj_pass(te, t, [(pA, B_WV_H, B_WV_L), (pB, B_WG_H, B_WG_L)], 0)
                        attn_proj_pass(te, t, [(pA, B_WV_H, B_WV_L), (pB, B_WG_H, B_WG_L)], 1)
                        if t + 1 < NT:
                            te.wait_ge(S.sB_x, 16 * (t + 2))
                            transpose8(te, pD, afs(PH + ((t + 1) % 3) * 1024)).then_inc(S.sB_tr, 1)
                        if t >= 1:
                            sg_ = t - 1
                            te.wait_ge(S.sB_zg, sg_ + 1)
                            te.wait_ge(S.sB_split, min(t + 2, NT))
                            transpose8(te, pD, afs(PH + 3072 + (sg_ % 2) * 1024)).then_inc(S.sB_zgtr, 1)
                        last = attn_proj_pass(te, t, [(pA, B_WV_H, B_WV_L), (pB, B_WG_H, B_WG_L)], 2)
                        last.then_inc(S.sB_vg, 1)
                    else:
                        sg_ = t - 1
                        te.wait_ge(S.sB_zg, sg_ + 1)
                        te.wait_ge(S.sB_split, NT)
                        transpose8(te, pD, afs(PH + 3072 + (sg_ % 2) * 1024)).then_inc(S.sB_zgtr, 1)
                    if t >= 1:
                        sg_ = t - 1
                        te.wait_ge(S.sB_zgsplit, sg_ + 1)
                        if sg_ >= 1:
                            te.wait_ge(S.sB_y, sg_)
                        lastw = None
                        for pi, zb in enumerate([ZG_H, ZG_H, ZG_L]):
                            wb = B_WO_H if pi != 1 else B_WO_L
                            for ki in range(KD):
                                lhs = ab[:, zb + ki * P: zb + (ki + 1) * P]
                                for no in range(2):
                                    lastw = te.matmul(
                                        pC[:, no * 512:(no + 1) * 512], lhs,
                                        ab[:, wb + ki * 1024 + no * 512: wb + ki * 1024 + (no + 1) * 512],
                                        start=(pi == 0 and ki == 0), stop=(pi == 2 and ki == KD - 1))
                        lastw.then_inc(S.sB_wo, 1)
                te.sem_inc(S.BAR, 1)
                te.wait_ge(S.BAR, bar_val(1))
            if NPHASE >= 4:
                te.wait_ge(S.sC_tern, 16 * 128)
                te.wait_ge(S.sC_intx, 1)
                transpose8(te, pC, afs(PH + 2048)).then_inc(S.sC_ixtr, 1)

                def ih_round(tt, r):
                    dst = pC if r % 2 == 0 else pD
                    if r == 0:
                        te.wait_ge(S.sC_inth, tt + 1)
                        if tt >= 1:
                            te.wait_ge(S.sC_ihT, 4 * (tt - 1) + 4)
                    elif r == 1 and tt >= 1:
                        te.wait_ge(S.sC_ihT, 4 * (tt - 1) + 4)
                    elif r >= 2:
                        te.wait_ge(S.sC_ihT, 4 * tt + r - 1)
                    lastr = None
                    for kk in range(8):
                        jc = r * 8 + kk
                        src = af[:, PERSIST + jc * P: PERSIST + (jc + 1) * P]
                        lastr = te.transpose(dst[:, kk * P:(kk + 1) * P], src, ident[:])
                    lastr.then_inc(S.sC_ihtr, 1)

                for t in range(NT + 1):
                    if t < NT:
                        te.wait_ge(S.sC_ixT, t + 1)
                        ixb = PH + 8192 + (t % 2) * 512
                        for oc in range(8):
                            if 8 * t + oc >= 2:
                                te.wait_ge(S.sC_sg, 8 * t + oc - 1)
                                te.wait_ge(S.sC_h, 8 * t + oc - 1)
                            po = (oc % 2) * 512
                            for ki in range(KD):
                                lhs = af[:, ixb + ki * 64: ixb + (ki + 1) * 64].bitcast(BF16)
                                g_rhs = ab[:, ki * 4096 + oc * 512: ki * 4096 + (oc + 1) * 512]
                                u_rhs = ab[:, 32768 + ki * 4096 + oc * 512: 32768 + ki * 4096 + (oc + 1) * 512]
                                te.matmul(pA[:, po:po + 512], lhs, g_rhs, start=(ki == 0), stop=(ki == KD - 1))
                                mm = te.matmul(pB[:, po:po + 512], lhs, u_rhs, start=(ki == 0), stop=(ki == KD - 1))
                            mm.then_inc(S.sC_gu, 1)
                            if t >= 1 and oc >= 4:
                                ih_round(t - 1, oc - 4)
                        if t + 1 < NT:
                            te.wait_ge(S.sC_intx, t + 2)
                            if t >= 1:
                                te.wait_ge(S.sC_ihT, 4 * (t - 1) + 3)
                            transpose8(te, pC, afs(PH + 2048)).then_inc(S.sC_ixtr, 1)
                    else:
                        for r in range(4):
                            ih_round(NT - 1, r)
                te.sem_inc(S.BAR, 1)
                te.wait_ge(S.BAR, bar_val(2))
                te.wait_ge(S.sD_tern, 16 * 64)
                for t in range(NT):
                    te.wait_ge(S.sD_ih, 16 * (t + 1))
                    if t >= 2:
                        te.wait_ge(S.sD_y2, t - 1)
                    ps = pA if t % 2 == 0 else pB
                    ihb = af[:, PH + (t % 2) * 2048: PH + (t % 2) * 2048 + 2048]
                    for ki in range(KF):
                        lhs = ihb[:, ki * 64:(ki + 1) * 64].bitcast(BF16)
                        for no in range(2):
                            mm = te.matmul(ps[:, no * 512:(no + 1) * 512], lhs,
                                           ab[:, ki * 1024 + no * 512: ki * 1024 + (no + 1) * 512],
                                           start=(ki == 0), stop=(ki == KF - 1))
                    mm.then_inc(S.sD_mm, 1)
                te.sem_inc(S.BAR, 1)
                te.wait_ge(S.BAR, bar_val(3))

        @block.vector
        def _(vv):
            for i in range(16):
                vv.wait_ge(S.sP_w, 16 * (i + 1))
                base = [A_WQ_H, A_WK_H][i // 8]
                lbase = [A_WQ_L, A_WK_L][i // 8]
                ci = i % 8
                src = afs(PH + (i % 2) * 1024)
                hi = ab[:, base + ci * 1024: base + (ci + 1) * 1024]
                lo = ab[:, lbase + ci * 1024: lbase + (ci + 1) * 1024]
                vv.tensor_copy(hi, src)
                vv.tensor_tensor(lo, src, hi, OP.subtract).then_inc(S.sP_wsplit, 1)
            vv.wait_ge(S.sA_tr, 1)
            vv.tensor_copy(ab[:, XT_H:XT_H + 1024], pD)
            vv.tensor_tensor(ab[:, XT_L:XT_L + 1024], pD,
                             ab[:, XT_H:XT_H + 1024], OP.subtract).then_inc(S.sA_split, 1)
            for t in range(NT):
                par = t % 2
                if t == TPB:
                    vv.wait_ge(S.sA_ksmm, TPB)
                    vv.tensor_copy(af[0:1, KSUM_BC:KSUM_BC + 1024], pC[0:1, :]).then_inc(S.sA_kscopy, 1)
                    vv.wait_ge(S.sA_ksbc, 1)
                    vv.tensor_copy(af[:, KSUM_BC:KSUM_BC + 1024], pC).then_inc(S.sA_ksbccopy, 1)
                q_sb = afs(PH + 2048 + par * 1024)
                k_sb = afs(PH + 4096 + par * 1024)
                vv.wait_ge(S.sA_evac, t + 1)
                if t + 1 < NT:
                    vv.wait_ge(S.sA_tr, t + 2)
                    xh2 = ab[:, XT_H + ((t + 1) % 2) * 1024: XT_H + ((t + 1) % 2) * 1024 + 1024]
                    xl2 = ab[:, XT_L + ((t + 1) % 2) * 1024: XT_L + ((t + 1) % 2) * 1024 + 1024]
                    vv.tensor_copy(xh2, pD)
                    vv.tensor_tensor(xl2, pD, xh2, OP.subtract).then_inc(S.sA_split, 1)
                vv.tensor_scalar(afs(PH + 6144), q_sb, 0.0, None, OP.min).then_inc(S.sA_minq, 1)
                vv.tensor_scalar(afs(PH + 8192), k_sb, 0.0, None, OP.min).then_inc(S.sA_mink, 1)
                vv.wait_ge(S.sA_expq, t + 1)
                vv.scalar_tensor_tensor(q_sb, q_sb, 0.0, afs(PH + 7168), OP.max, OP.add).then_inc(S.sA_q1, 1)
                vv.wait_ge(S.sA_expk, t + 1)
                vv.scalar_tensor_tensor(k_sb, k_sb, 0.0, afs(PH + 9216), OP.max, OP.add).then_inc(S.sA_k1, 1)
                vv.tensor_tensor(afs(PH + 6144), q_sb, k_sb, OP.mult)
                vv.tensor_reduce(af[:, QK_ALL + t * 16: QK_ALL + (t + 1) * 16],
                                 afs(PH + 6144).rearrange("p (h d) -> p h d", h=NH),
                                 AX.X, OP.add)
                if t == 0:
                    vv.memset(racc128[:], 0.0)
                if t < 32:
                    for j in range(3 * t, 3 * t + 3):
                        m = j // 32
                        vv.wait_ge(S.sF_dma, 16 * (j + 1))
                        vred(racc128[:, 3:4], wslot(j),
                             AX.X, OP.add, absval=True)
                        vtt(racc128[:, m:m + 1], racc128[:, m:m + 1],
                            racc128[:, 3:4], OP.add, fence=(S.sF_vr, j + 1))
                    if t == 31:
                        vv.sem_inc(S.sF_sum, 1)
                if t == 33:
                    vv.wait_ge(S.sF_gm, 3)
                    for m in range(3):
                        src = racc[0:1, m:m + 1]
                        vts(swu[0:1, m:m + 1], src, RMEAN_W, 1e-5, OP.mult, OP.max)
                        vrecip(swu[0:1, m:m + 1], swu[0:1, m:m + 1])
                        vrecip(swu[0:1, 4 + m:5 + m], swu[0:1, m:m + 1])
                        vv.engine_nop().then_inc(S.sF_scal, 1)
                if t >= 34:
                    for i in range(4 * (t - 34), min(ffn_ntile, 4 * (t - 33))):
                        vv.wait_ge(S.sF_s1, i + 1)
                        if i >= 2:
                            vv.wait_ge(S.st_tern, 16 * (i - 1))
                        ws = wslot(i)
                        vv.tensor_scalar(ws, ws, CM, 1.0, OP.subtract, OP.min)
                        vv.tensor_scalar(tfr[:, (i % 2) * 1024:(i % 2 + 1) * 1024],
                                         ws, -1.0, None, OP.max).then_inc(S.sF_vq, 1)
            vv.wait_ge(S.sA_ksmm, NT)
            vv.tensor_copy(af[0:1, KSUM_BC + 1024:KSUM_BC + 2048], pC[0:1, :]).then_inc(S.sA_kscopy, 1)
            vv.wait_ge(S.sA_ksbc, 2)
            vv.tensor_copy(af[:, KSUM_BC + 1024:KSUM_BC + 2048], pC).then_inc(S.sA_ksbccopy, 1)
            vv.sem_inc(S.BAR, 1)
            vv.wait_ge(S.BAR, bar_val(0))
            if NPHASE >= 2:
                for i in range(24):
                    vv.wait_ge(S.sB_w, 16 * (i + 1))
                    base = [B_WV_H, B_WG_H, B_WO_H][i // 8]
                    lbase = [B_WV_L, B_WG_L, B_WO_L][i // 8]
                    ci = i % 8
                    src = afs(PH + (i % 2) * 1024)
                    vv.tensor_copy(ab[:, base + ci * 1024: base + (ci + 1) * 1024], src)
                    vv.tensor_tensor(ab[:, lbase + ci * 1024: lbase + (ci + 1) * 1024], src,
                                     ab[:, base + ci * 1024: base + (ci + 1) * 1024],
                                     OP.subtract).then_inc(S.sB_wsplit, 1)
                vv.wait_ge(S.sF_bc, 224)
                vv.wait_ge(S.sB_tr, 1)
                xh = ab[:, XT_H:XT_H + 1024]
                xl = ab[:, XT_L:XT_L + 1024]
                vv.tensor_copy(xh, pD)
                vv.tensor_tensor(xl, pD, xh, OP.subtract).then_inc(S.sB_split, 1)
                for t in range(NT + 1):
                    if t < NT:
                        par = t % 2
                        b = t // TPB
                        vv.wait_ge(S.sB_q, 16 * (t + 1))
                        qin = afs(PH + 3072 + par * 1024)
                        tmpP = afs(PH + 5120)
                        vv.tensor_tensor(tmpP, qin, af[:, KSUM_BC + b * 1024: KSUM_BC + (b + 1) * 1024],
                                         OP.mult).then_inc(S.sB_P, 1)
                        vred(tv[:, NRM16:NRM16 + 16],
                             tmpP.rearrange("p (h d) -> p h d", h=NH), AX.X, OP.add)
                        vts(tv[:, NRM16:NRM16 + 16], tv[:, NRM16:NRM16 + 16],
                            1e-6, None, OP.add)
                        vrecip(tv[:, REC16:REC16 + 16], tv[:, NRM16:NRM16 + 16])
                        vtt(tv[:, R16:R16 + 16], af[:, QK_ALL + t * 16: QK_ALL + (t + 1) * 16],
                            tv[:, REC16:REC16 + 16], OP.mult)
                        if t + 1 < NT:
                            vv.wait_ge(S.sB_tr, t + 2)
                            vv.wait_ge(S.sB_vg, t)
                            xh = ab[:, XT_H + ((t + 1) % 2) * 1024: XT_H + ((t + 1) % 2) * 1024 + 1024]
                            xl = ab[:, XT_L + ((t + 1) % 2) * 1024: XT_L + ((t + 1) % 2) * 1024 + 1024]
                            vv.tensor_copy(xh, pD)
                            vv.tensor_tensor(xl, pD, xh, OP.subtract).then_inc(S.sB_split, 1)
                        vv.wait_ge(S.sB_vg, t + 1)
                        z = afs(PH + 6144)
                        rb = tv[:, R16:R16 + 16].unsqueeze(2).broadcast_to([P, NH, DH])
                        vv.scalar_tensor_tensor(z.rearrange("p (h d) -> p h d", h=NH),
                                                pA.rearrange("p (h d) -> p h d", h=NH),
                                                1.0, rb, OP.mult, OP.mult,
                                                accum_out=tvc(ZSUM)).then_inc(S.sB_z, 1)
                        vv.wait_ge(S.sB_z, t + 1)
                        if t >= 1:
                            sg_e = t - 1
                            vv.wait_ge(S.sB_zgtr, sg_e + 1)
                            if sg_e >= 1:
                                vv.wait_ge(S.sB_wo, sg_e)
                            vv.tensor_copy(ab[:, ZG_H:ZG_H + 1024], pD)
                            vv.tensor_tensor(ab[:, ZG_L:ZG_L + 1024], pD, ab[:, ZG_H:ZG_H + 1024],
                                             OP.subtract).then_inc(S.sB_zgsplit, 1)
                        vts(tvc(MUz), tvc(ZSUM), R1024, None, OP.mult)
                        vv.wait_ge(S.sB_sqz, t + 1)
                        vts(tvc(AAz), tvc(MUz), tvc(MUz), 1e-5, OP.mult, OP.subtract)
                        vts(tvc(VVz), tvc(ZSS), R1024, tvc(AAz), OP.mult, OP.subtract,
                            fence=(S.sB_varz, t + 1))
                        vv.wait_ge(S.sB_sqrtz, t + 1)
                        newton_rsqrt(tvc(VVz), tvc(SVz), tvc(R0z), tvc(RRz), tvc(AAz), tvc(BBz), tvc(RSz))
                        vv.tensor_scalar(tmpP, z, tvc(MUz), tvc(RSz), OP.subtract, OP.mult)
                        vv.tensor_tensor(tmpP, tmpP, af[:, GB_BC:GB_BC + 1024], OP.mult)
                        vv.tensor_tensor(tmpP, tmpP, af[:, GB_BC + 1024:GB_BC + 2048], OP.add)
                        vv.wait_ge(S.sB_sg, t + 1)
                        vv.tensor_tensor(qin, tmpP, afs(PH + 8192), OP.mult).then_inc(S.sB_zg, 1)
                    if t == NT:
                        sg_e = t - 1
                        vv.wait_ge(S.sB_zgtr, sg_e + 1)
                        vv.wait_ge(S.sB_wo, sg_e)
                        vv.tensor_copy(ab[:, ZG_H:ZG_H + 1024], pD)
                        vv.tensor_tensor(ab[:, ZG_L:ZG_L + 1024], pD, ab[:, ZG_H:ZG_H + 1024],
                                         OP.subtract).then_inc(S.sB_zgsplit, 1)
                    if t >= 1:
                        sg_ = t - 1
                        tmp1 = afs(PH + 7168)
                        vv.wait_ge(S.sB_wo, sg_ + 1)
                        vv.scalar_tensor_tensor(tmp1, afs(PH + (sg_ % 3) * 1024), 1.0, pC,
                                                OP.mult, OP.add, accum_out=tvc(YSUM)).then_inc(S.sB_y, 1)
                        vv.wait_ge(S.sB_y, sg_ + 1)
                        vts(tvc(MUy), tvc(YSUM), R1024, None, OP.mult)
                        vv.wait_ge(S.sB_sqy, sg_ + 1)
                        vts(tvc(AAy), tvc(MUy), tvc(MUy), 1e-5, OP.mult, OP.subtract)
                        vts(tvc(VVy), tvc(YSS), R1024, tvc(AAy), OP.mult, OP.subtract,
                            fence=(S.sB_vary, sg_ + 1))
                        vv.wait_ge(S.sB_sqrty, sg_ + 1)
                        newton_rsqrt(tvc(VVy), tvc(SVy), tvc(R0y), tvc(RRy), tvc(AAy), tvc(BBy), tvc(RSy))
                        if sg_ >= 2:
                            vv.wait_ge(S.st_y, 16 * (sg_ - 1))
                        vv.tensor_scalar(afs(PH + 9216 + (sg_ % 2) * 1024), tmp1, tvc(MUy), tvc(RSy),
                                         OP.subtract, OP.mult).then_inc(S.sB_y1, 1)
                vv.sem_inc(S.BAR, 1)
                vv.wait_ge(S.BAR, bar_val(1))
            if NPHASE >= 4:
                def quantA(tq, part):
                    par = tq % 2
                    yb = afs(PH + par * 1024)
                    if part == 0:
                        vv.wait_ge(S.sC_y, 16 * (tq + 1))
                        vred(tvc(MX), yb, AX.X, OP.max, absval=True)
                        vts(tvc(MC), tvc(MX), 1e-5, None, OP.max)
                    else:
                        vrecip(tvc(RECM), tvc(MC))
                        vts(tvc(SXV + 26 * par), tvc(RECM), 127.0, None, OP.mult)

                def quantB(tq, part):
                    par = tq % 2
                    yb = afs(PH + par * 1024)
                    intx = afs(PH + 2048)
                    if part == 0:
                        if tq >= 2:
                            vv.wait_ge(S.sC_sg, 8 * (tq - 2) + 8)
                        vrecip(tvc(RECM), tvc(SXV + 26 * par))
                        vtt(tvc(CGc + 22 * par), tvc(RECM), uwbc[:, 0:1], OP.mult)
                        vtt(tvc(CUc + 22 * par), tvc(RECM), uwbc[:, 1:2], OP.mult)
                    elif part == 1:
                        if tq >= 1:
                            vv.wait_ge(S.sC_ixtr, tq)
                        vv.tensor_scalar(intx, yb, tvc(SXV + 26 * par), CM, OP.mult, OP.add)
                    elif part == 2:
                        vv.tensor_scalar(intx, intx, CM, 127.0, OP.subtract, OP.min)
                    else:
                        vv.tensor_scalar(intx, intx, -128.0, None, OP.max).then_inc(S.sC_intx, 1)

                def y_quant(tq):
                    quantA(tq, 0)
                    quantA(tq, 1)
                    for pp in range(4):
                        quantB(tq, pp)

                y_quant(0)
                for t in range(NT + 1):
                    if dbg == "c1" and t >= 1:
                        vv.wait_ge(S.sDbg, 16 + 64 * t)
                    if t < NT:
                        if t == 0:
                            y_quant(1)
                        for oc in range(8):
                            po = (oc % 2) * 512
                            vv.wait_ge(S.sC_sg, 8 * t + oc + 1)
                            h_oc = af[:, PH + 3072 + oc * 512: PH + 3072 + (oc + 1) * 512]
                            vv.scalar_tensor_tensor(h_oc, pB[:, po:po + 512], tvc(CUc + 22 * (t % 2)),
                                                    afs(PH + 7168 + (oc % 2) * 512, 512),
                                                    OP.mult, OP.mult).then_inc(S.sC_h, 1)
                            vv.tensor_reduce(tv[:, HM8 + oc:HM8 + oc + 1], h_oc, AX.X,
                                             OP.max, apply_absolute_value=True)
                            if t >= 2 and oc < 4:
                                quantB(t + 1, oc)
                            if t >= 1 and 4 <= oc < 6 and t + 2 < NT:
                                quantA(t + 2, oc - 4)
                            if t >= 1 and oc >= 4:
                                r = oc - 4
                                vv.wait_ge(S.sC_ihtr, 4 * (t - 1) + r + 1)
                                if r == 0 and t >= 2:
                                    vv.wait_ge(S.st_ih, 16 * (t - 1))
                                src = pC if r % 2 == 0 else pD
                                vv.tensor_copy(af[:, PH + 9216 + r * 512: PH + 9216 + (r + 1) * 512].bitcast(BF16),
                                               src).then_inc(S.sC_ihT, 1)
                        vv.engine_nop()
                        vred(tvc(HM), tv[:, HM8:HM8 + 8], AX.X, OP.max)
                        vts(tvc(MH), tvc(HM), 1e-5, None, OP.max)
                        vrecip(tvc(RH), tvc(MH))
                        vts(tvc(SH), tvc(RH), 127.0, None, OP.mult)
                        vrecip(tvc(REC3), tvc(SH))
                        vtt(cd_all[:, t:t + 1], tvc(REC3), uwbc[:, 2:3], OP.mult)
                        if t >= 1:
                            vv.wait_ge(S.sC_ihtr, 4 * t)
                        vv.engine_nop().then_inc(S.sC_sh, 1)
                        ih = af[:, PERSIST:PERSIST + 4096]
                        vv.wait_ge(S.sC_ih1, t + 1)
                        vv.tensor_scalar(ih, ih, CM, 127.0, OP.subtract, OP.min)
                        vv.tensor_scalar(ih, ih, -128.0, None, OP.max).then_inc(S.sC_inth, 1)
                        if t == 0 and NT > 2:
                            y_quant(2)
                    else:
                        for r in range(4):
                            vv.wait_ge(S.sC_ihtr, 4 * (NT - 1) + r + 1)
                            if r == 0:
                                vv.wait_ge(S.st_ih, 16 * (NT - 1))
                            src = pC if r % 2 == 0 else pD
                            vv.tensor_copy(af[:, PH + 9216 + r * 512: PH + 9216 + (r + 1) * 512].bitcast(BF16),
                                           src).then_inc(S.sC_ihT, 1)
                vv.sem_inc(S.BAR, 1)
                vv.wait_ge(S.BAR, bar_val(2))
                for t in range(NT):
                    par = t % 2
                    ps = pA if par == 0 else pB
                    vv.wait_ge(S.sD_mm, t + 1)
                    vv.wait_ge(S.sD_y, 16 * (t + 1))
                    y2 = afs(PH + 6144)
                    vv.scalar_tensor_tensor(y2, ps, cd_all[:, t:t + 1],
                                            afs(PH + 4096 + par * 1024),
                                            OP.mult, OP.add, accum_out=tvc(YSUM)).then_inc(S.sD_y2, 1)
                    vv.wait_ge(S.sD_y2, t + 1)
                    vts(tvc(MUy), tvc(YSUM), R1024, None, OP.mult)
                    vv.wait_ge(S.sD_sq, t + 1)
                    vts(tvc(VVy), tvc(YSS), R1024, None, OP.mult)
                    vtt(tvc(AAy), tvc(MUy), tvc(MUy), OP.mult)
                    vtt(tvc(VVy), tvc(VVy), tvc(AAy), OP.subtract)
                    vts(tvc(VVy), tvc(VVy), 1e-5, None, OP.add, fence=(S.sD_var, t + 1))
                    vv.wait_ge(S.sD_sqrt, t + 1)
                    newton_rsqrt(tvc(VVy), tvc(SVy), tvc(R0y), tvc(RRy), tvc(AAy), tvc(BBy), tvc(RSy))
                    if t >= 2:
                        vv.wait_ge(S.st_out, 16 * (t - 1))
                    vv.tensor_scalar(afs(PH + 7168 + par * 1024), y2, tvc(MUy), tvc(RSy),
                                     OP.subtract, OP.mult).then_inc(S.sD_out, 1)
                vv.sem_inc(S.BAR, 1)
                vv.wait_ge(S.BAR, bar_val(3))

        @block.scalar
        def _(sc):
            for t in range(NT):
                par = t % 2
                sc.wait_ge(S.sA_qk, t + 1)
                if t >= 2:
                    sc.wait_ge(S.st_q, 16 * (t - 1))
                    sc.wait_ge(S.sA_ksmm, t - 1)
                sc.activation(afs(PH + 2048 + par * 1024), pA, AF.Identity)
                sc.activation(afs(PH + 4096 + par * 1024), pB, AF.Identity).then_inc(S.sA_evac, 1)
                sc.wait_ge(S.sA_minq, t + 1)
                sc.activation(afs(PH + 7168), afs(PH + 6144), AF.Exp).then_inc(S.sA_expq, 1)
                sc.wait_ge(S.sA_mink, t + 1)
                sc.activation(afs(PH + 9216), afs(PH + 8192), AF.Exp).then_inc(S.sA_expk, 1)
                sc.wait_ge(S.sA_q1, t + 1)
                sc.dma_start(out=q_scr[t * P:(t + 1) * P, :],
                             in_=afs(PH + 2048 + par * 1024)).then_inc(S.st_q, 16)
                if t < 32:
                    for j in range(3 * t, 3 * t + 3):
                        if j >= 3:
                            sc.wait_ge(S.sF_vr, j - 2)
                        sc.dma_start(out=wslot(j),
                                     in_=ffn_tile_src(j)).then_inc(S.sF_dma, 16)
                if t >= 34:
                    for i in range(4 * (t - 34), min(ffn_ntile, 4 * (t - 33))):
                        m = i // 32
                        if i % 32 == 0:
                            sc.wait_ge(S.sF_bc, 224 + 256 * (m + 1))
                        sc.wait_ge(S.sF_qdma, 16 * (i + 1))
                        sc.activation(wslot(i),
                                      wslot(i),
                                      AF.Identity, scale=swbc[:, m:m + 1],
                                      bias=cmb[:]).then_inc(S.sF_s1, 1)
            sc.wait_ge(S.st_q, 16 * NT)
            sc.sem_inc(S.BAR, 1)
            sc.wait_ge(S.BAR, bar_val(0))
            if NPHASE >= 2:
                for t in range(NT + 1):
                    if t < NT:
                        sc.wait_ge(S.sB_z, t + 1)
                        sc.activation(afs(PH + 8192), afs(PH + 6144), AF.Square,
                                      accum_out=tvc(ZSS)).then_inc(S.sB_sqz, 1)
                        sc.wait_ge(S.sB_varz, t + 1)
                        sc.activation(tvc(SVz), tvc(VVz), AF.Sqrt).then_inc(S.sB_sqrtz, 1)
                        sc.wait_ge(S.sB_vg, t + 1)
                        if t >= 1:
                            sc.wait_ge(S.sB_zg, t)
                        sc.activation(afs(PH + 8192), pB, AF.Silu).then_inc(S.sB_sg, 1)
                    if t >= 1:
                        sg_ = t - 1
                        sc.wait_ge(S.sB_y, sg_ + 1)
                        sc.activation(afs(PH + 6144), afs(PH + 7168), AF.Square,
                                      accum_out=tvc(YSS)).then_inc(S.sB_sqy, 1)
                        sc.wait_ge(S.sB_vary, sg_ + 1)
                        sc.activation(tvc(SVy), tvc(VVy), AF.Sqrt).then_inc(S.sB_sqrty, 1)
                        sc.wait_ge(S.sB_y1, sg_ + 1)
                        sc.dma_start(out=y_scr[sg_ * P:(sg_ + 1) * P, :],
                                     in_=afs(PH + 9216 + (sg_ % 2) * 1024)).then_inc(S.st_y, 16)
                sc.wait_ge(S.st_y, 16 * NT)
                sc.sem_inc(S.BAR, 1)
                sc.wait_ge(S.BAR, bar_val(1))
            if NPHASE >= 4:
                sc.wait_ge(S.sC_ixtr, 1)
                sc.activation(af[:, PH + 8192:PH + 8704].bitcast(BF16), pC,
                              AF.Identity).then_inc(S.sC_ixT, 1)
                for t in range(NT + 1):
                    if t < NT:
                        for oc in range(8):
                            po = (oc % 2) * 512
                            sc.wait_ge(S.sC_gu, 8 * t + oc + 1)
                            if 8 * t + oc >= 2:
                                sc.wait_ge(S.sC_h, 8 * t + oc - 1)
                            sc.activation(afs(PH + 7168 + (oc % 2) * 512, 512),
                                          pA[:, po:po + 512], AF.Silu,
                                          scale=tvc(CGc + 22 * (t % 2))).then_inc(S.sC_sg, 1)
                        if t + 1 < NT:
                            sc.wait_ge(S.sC_ixtr, t + 2)
                            dst = PH + 8192 + ((t + 1) % 2) * 512
                            sc.activation(af[:, dst:dst + 512].bitcast(BF16), pC,
                                          AF.Identity).then_inc(S.sC_ixT, 1)
                        sc.wait_ge(S.sC_sh, t + 1)
                        sc.activation(af[:, PERSIST:PERSIST + 4096],
                                      af[:, PH + 3072:PH + 7168], AF.Identity,
                                      scale=tvc(SH), bias=cmb[:]).then_inc(S.sC_ih1, 1)
                    if t >= 1:
                        sg_ = t - 1
                        sc.wait_ge(S.sC_ihT, 4 * (sg_ + 1))
                        sc.dma_start(out=ih_scr[sg_ * P:(sg_ + 1) * P, :],
                                     in_=af[:, PH + 9216:PH + 11264].bitcast(BF16)).then_inc(S.st_ih, 16)
                sc.wait_ge(S.st_ih, 16 * NT)
                sc.sem_inc(S.BAR, 1)
                sc.wait_ge(S.BAR, bar_val(2))
                for t in range(NT):
                    par = t % 2
                    sc.wait_ge(S.sD_y2, t + 1)
                    sc.activation(afs(PH + 9216), afs(PH + 6144), AF.Square,
                                  accum_out=tvc(YSS)).then_inc(S.sD_sq, 1)
                    sc.wait_ge(S.sD_var, t + 1)
                    sc.activation(tvc(SVy), tvc(VVy), AF.Sqrt).then_inc(S.sD_sqrt, 1)
                    sc.wait_ge(S.sD_out, t + 1)
                    sc.dma_start(out=out_h[t * P:(t + 1) * P, :],
                                 in_=afs(PH + 7168 + par * 1024)).then_inc(S.st_out, 16)
                sc.wait_ge(S.st_out, 16 * NT)
                sc.sem_inc(S.BAR, 1)
                sc.wait_ge(S.BAR, bar_val(3))

        @block.gpsimd
        def _(gp):
            gp.memset(ident[:], 0.0)
            gp.affine_select(out=ident[:], in_=ident[:], compare_op=OP.not_equal,
                             fill=1.0, base=0, pattern=[[-1, P]], channel_multiplier=1)
            gp.memset(ones[:], 1.0)
            gp.memset(onescol[:], 1.0)
            gp.memset(racc[:], 0.0)
            gp.memset(cmb[:], CM)
            gp.sem_inc(S.sG_init, 1)
            fbc = [0]

            def pdouble(tens, c0, c1):
                k = 1
                while k < P:
                    n = min(k, P - k)
                    gp.dma_start(out=tens[k:k + n, c0:c1],
                                 in_=tens[0:n, c0:c1]).then_inc(S.sF_bc, 16)
                    fbc[0] += 16
                    gp.wait_ge(S.sF_bc, fbc[0])
                    k *= 2

            gp.wait_ge(S.s_gb, 32)
            pdouble(af, GB_BC, GB_BC + 1024)
            pdouble(af, GB_BC + 1024, GB_BC + 2048)
            gp.wait_ge(S.sF_sum, 1)
            for m in range(3):
                gp.tensor_reduce(racc[0:1, m:m + 1], racc128[:, m:m + 1],
                                 AX.C, OP.add).then_inc(S.sF_gm, 1)
            for m in range(3):
                gp.wait_ge(S.sF_scal, m + 1)
                gp.dma_start(out=swbc[0:1, m:m + 1],
                             in_=swu[0:1, m:m + 1]).then_inc(S.sF_bc, 16)
                fbc[0] += 16
                gp.wait_ge(S.sF_bc, fbc[0])
                pdouble(swbc[:], m, m + 1)
                gp.dma_start(out=uwbc[0:1, m:m + 1],
                             in_=swu[0:1, 4 + m:5 + m]).then_inc(S.sF_bc, 16)
                fbc[0] += 16
                gp.wait_ge(S.sF_bc, fbc[0])
                pdouble(uwbc[:], m, m + 1)
            for i0 in range(3):
                gp.dma_start(out=wslot(i0),
                             in_=ffn_tile_src(i0)).then_inc(S.sF_qdma, 16)
            for i in range(ffn_ntile):
                gp.wait_ge(S.sF_vq, i + 1)
                gp.dma_start(out=ffn_tile_dst(i),
                             in_=tfr[:, (i % 2) * 1024:(i % 2 + 1) * 1024]).then_inc(S.st_tern, 16)
                if i + 3 < ffn_ntile:
                    gp.dma_start(out=wslot(i + 3),
                                 in_=ffn_tile_src(i + 3)).then_inc(S.sF_qdma, 16)
            gp.wait_ge(S.st_tern, 16 * ffn_ntile)
            gp.sem_inc(S.BAR, 1)
            gp.wait_ge(S.BAR, bar_val(0))
            if NPHASE >= 2:
                gp.sem_inc(S.BAR, 1)
                gp.wait_ge(S.BAR, bar_val(1))
            if NPHASE >= 4:
                if dbg == "c1":
                    gp.wait_ge(S.sC_tern, 16 * 128)
                    gp.dma_start(out=dbg2_h[:, :], in_=ab[:, :]).then_inc(S.sDbg, 16)
                    gp.wait_ge(S.sDbg, 16)
                    for t in range(NT):
                        gp.wait_ge(S.sC_inth, t + 1)
                        r0, r1 = t * P, (t + 1) * P
                        gp.dma_start(out=dbg_h[r0:r1, 0:1024], in_=af[:, PH + 2048:PH + 3072]).then_inc(S.sDbg, 16)
                        gp.dma_start(out=dbg_h[r0:r1, 1024:5120], in_=af[:, PH + 3072:PH + 7168]).then_inc(S.sDbg, 16)
                        gp.dma_start(out=dbg_h[r0:r1, 5120:9216], in_=af[:, PERSIST:PERSIST + 4096]).then_inc(S.sDbg, 16)
                        gp.dma_start(out=dbg_h[r0:r1, 9216:9312], in_=tv[:]).then_inc(S.sDbg, 16)
                    gp.wait_ge(S.sDbg, 16 + 64 * NT)
                gp.sem_inc(S.BAR, 1)
                gp.wait_ge(S.BAR, bar_val(2))
                gp.sem_inc(S.BAR, 1)
                gp.wait_ge(S.BAR, bar_val(3))
            if stage == 1:
                for t in range(NT):
                    gp.dma_start(out=out_h[t * P:(t + 1) * P, :],
                                 in_=q_scr[t * P:(t + 1) * P, :]).then_inc(S.st_out, 16)
                gp.wait_ge(S.st_out, 16 * NT)
            elif stage == 2:
                for t in range(NT):
                    gp.dma_start(out=out_h[t * P:(t + 1) * P, :],
                                 in_=y_scr[t * P:(t + 1) * P, :]).then_inc(S.st_out, 16)
                gp.wait_ge(S.st_out, 16 * NT)

    return nc


def kernel(**inputs):
    x = np.ascontiguousarray(np.asarray(inputs["L0_attn_hidden_states"], dtype=np.float32))
    B, Sq, Dm = x.shape
    assert (B, Sq, Dm) == (16, 4096, 1024)
    w = {k: np.ascontiguousarray(np.asarray(inputs[k], dtype=np.float32))
         for k in ["L0_attn_Wq", "L0_attn_Wk", "L0_attn_Wv", "L0_attn_Wg", "L0_attn_Wo",
                   "L0_ffn_w_gate", "L0_ffn_w_up", "L0_ffn_w_down"]}
    gam = np.ascontiguousarray(np.asarray(inputs["L0_attn_gamma"], dtype=np.float32).reshape(1, Dm))
    bet = np.ascontiguousarray(np.asarray(inputs["L0_attn_beta"], dtype=np.float32).reshape(1, Dm))
    nc = build(stage=3)
    in_maps = []
    for c in range(8):
        m = dict(w)
        m["L0_attn_gamma"] = gam
        m["L0_attn_beta"] = bet
        m["L0_attn_hidden_states"] = x[2 * c:2 * c + 2].reshape(2 * Sq, Dm)
        in_maps.append(m)
    res = run_bass_kernel_spmd(nc, in_maps, core_ids=list(range(8)))
    out = np.concatenate([res.results[c]["out"].reshape(2, Sq, Dm) for c in range(8)], axis=0)
    return out


# version 28 (final)
# speedup vs baseline: 1.0832x; 1.0832x over previous
import numpy as np
import concourse.bass as bass
import concourse.mybir as mybir
from concourse.bass_utils import run_bass_kernel_spmd
from contextlib import ExitStack

F32 = mybir.dt.float32
BF16 = mybir.dt.bfloat16
OP = mybir.AluOpType
AF = mybir.ActivationFunctionType
AX = mybir.AxisListType

P = 128
D = 1024
DFF = 4096
NH, DH = 16, 64
NB = 2
TPB = 32
NT = NB * TPB
KD = D // P
KF = DFF // P
CM = 12582912.0
R1024 = 1.0 / 1024.0
DBG_B = None
RMEAN_W = 1.0 / (DFF * D)

AF_COLS = 16384
PH = 0
PERSIST = 11264
QK_ALL = PERSIST
KSUM_BC = PERSIST + 1024
GB_BC = PERSIST + 3072
AB_COLS = 65536
XT_H = 57344
XT_L = 59392
ZG_H = 61440
ZG_L = 62464


def build(stage=3, dbg=None):
    nc = bass.Bass("TRN2", target_bir_lowering=False)
    ap = lambda t: t.ap()
    x_h = ap(nc.dram_tensor("L0_attn_hidden_states", [NT * P, D], F32, kind="ExternalInput"))
    wq_h = ap(nc.dram_tensor("L0_attn_Wq", [D, D], F32, kind="ExternalInput"))
    wk_h = ap(nc.dram_tensor("L0_attn_Wk", [D, D], F32, kind="ExternalInput"))
    wv_h = ap(nc.dram_tensor("L0_attn_Wv", [D, D], F32, kind="ExternalInput"))
    wg_h = ap(nc.dram_tensor("L0_attn_Wg", [D, D], F32, kind="ExternalInput"))
    wo_h = ap(nc.dram_tensor("L0_attn_Wo", [D, D], F32, kind="ExternalInput"))
    gam_h = ap(nc.dram_tensor("L0_attn_gamma", [1, D], F32, kind="ExternalInput"))
    bet_h = ap(nc.dram_tensor("L0_attn_beta", [1, D], F32, kind="ExternalInput"))
    wgate_h = ap(nc.dram_tensor("L0_ffn_w_gate", [DFF, D], F32, kind="ExternalInput"))
    wup_h = ap(nc.dram_tensor("L0_ffn_w_up", [DFF, D], F32, kind="ExternalInput"))
    wdown_h = ap(nc.dram_tensor("L0_ffn_w_down", [D, DFF], F32, kind="ExternalInput"))
    out_h = ap(nc.dram_tensor("out", [NT * P, D], F32, kind="ExternalOutput"))
    q_scr = ap(nc.dram_tensor("q_scr", [NT * P, D], F32, kind="Internal"))
    y_scr = ap(nc.dram_tensor("y_scr", [NT * P, D], F32, kind="Internal"))
    ih_scr = ap(nc.dram_tensor("ih_scr", [NT * P, DFF], BF16, kind="Internal"))
    tg_scr = ap(nc.dram_tensor("tg_scr", [DFF, D], BF16, kind="Internal"))
    tu_scr = ap(nc.dram_tensor("tu_scr", [DFF, D], BF16, kind="Internal"))
    td_scr = ap(nc.dram_tensor("td_scr", [D, DFF], BF16, kind="Internal"))
    dbg_h = ap(nc.dram_tensor("dbg", [NT * P, 9344], F32, kind="ExternalOutput")) if dbg else None
    dbg2_h = ap(nc.dram_tensor("dbg2", [P, AB_COLS], BF16, kind="ExternalOutput")) if dbg else None
    ffn_w_h = [wgate_h, wup_h, wdown_h]
    ffn_scr = [tg_scr, tu_scr, td_scr]

    with ExitStack() as ctx:
        e = ctx.enter_context
        af_t = e(nc.sbuf_tensor("af", [P, AF_COLS], F32))
        ab_t = e(nc.sbuf_tensor("ab", [P, AB_COLS], BF16))
        af = af_t[:]
        ab = ab_t[:]
        ident = e(nc.sbuf_tensor("ident", [P, P], F32))
        ones = e(nc.sbuf_tensor("ones", [P, 1], F32))
        onescol = e(nc.sbuf_tensor("onescol", [1, P], F32))
        racc = e(nc.sbuf_tensor("racc", [1, 8], F32))
        racc128 = e(nc.sbuf_tensor("racc128", [P, 4], F32))
        swu = e(nc.sbuf_tensor("swu", [1, 8], F32))
        swbc = e(nc.sbuf_tensor("swbc", [P, 4], F32))
        uwbc = e(nc.sbuf_tensor("uwbc", [P, 4], F32))
        wfr = e(nc.sbuf_tensor("wfr", [P, 2048], F32))
        tfr = e(nc.sbuf_tensor("tfr", [P, 2048], BF16))
        cd_all = e(nc.sbuf_tensor("cd_all", [P, NT], F32))
        cmb = e(nc.sbuf_tensor("cmb", [P, 1], F32))
        tv = e(nc.sbuf_tensor("tv", [P, 96], F32))
        MX, MC, RECM, SXV, CGc, CUc = 0, 1, 2, 3, 4, 5
        ZSUM, MUz, ZSS, VVz, SVz, R0z, RRz, AAz, BBz, RSz = 6, 7, 8, 9, 10, 11, 12, 13, 14, 15
        YSUM, MUy, YSS, VVy, SVy, R0y, RRy, AAy, BBy, RSy = 16, 17, 18, 19, 20, 21, 22, 23, 24, 25
        HM8 = 32
        HM, MH, RH, SH, REC3 = 40, 41, 42, 43, 44
        NRM16 = 48
        REC16 = 64
        R16 = 80
        ps_a = e(nc.psum_tensor("psA", [P, 1024], F32))
        ps_b = e(nc.psum_tensor("psB", [P, 1024], F32))
        ps_c = e(nc.psum_tensor("psC", [P, 1024], F32))
        ps_d = e(nc.psum_tensor("psD", [P, 1024], F32))
        pA, pB, pC, pD = ps_a[:], ps_b[:], ps_c[:], ps_d[:]

        sems = {}
        for name in [
            "BAR", "sG_init", "sP_w", "sP_wsplit", "sA_x", "sA_tr", "sA_split",
            "sA_evac", "sA_qk", "sA_minq", "sA_mink", "sA_expq", "sA_expk",
            "sA_q1", "sA_k1", "sA_ksmm", "sA_kscopy", "sA_ksbc", "sA_ksbccopy",
            "st_q", "sF_dma", "sF_gm", "sF_sum", "sF_scal", "sF_bc", "sF_qdma",
            "st_tern", "s_gb", "sF_vr", "sB_w", "sB_wsplit", "sB_x", "sB_q", "sB_tr",
            "sB_split", "sB_P", "sB_vg", "sB_z", "sB_sqz", "sB_varz", "sB_sqrtz",
            "sB_sg", "sB_zg", "sB_zgtr", "sB_zgsplit", "sB_wo", "sB_y", "sB_sqy",
            "sB_vary", "sB_sqrty", "sB_y1", "st_y", "sC_tern", "sC_y", "sC_intx",
            "sC_ixtr", "sC_ixT", "sC_gu", "sC_sg", "sC_h", "sC_inth", "sC_ihtr",
            "sC_ihT", "st_ih", "sD_tern", "sD_ih", "sD_y", "sD_mm", "sD_y2",
            "sD_sq", "sD_var", "sD_sqrt", "sD_out", "st_out", "sHZ", "sDbg", "sF_gp", "sF_s1", "sF_vq", "sC_ih1", "sC_sh",
        ]:
            sems[name] = e(nc.semaphore(name))
        S = type("S", (), sems)

        NPHASE = {1: 1, 2: 2, 3: 4}[stage]

        def bar_val(p):
            return 5 * (p + 1)

        def attn_proj_pass(te, tau, projs, pi):
            par = tau % 2
            xh = ab[:, XT_H + par * 1024: XT_H + par * 1024 + 1024]
            xl = ab[:, XT_L + par * 1024: XT_L + par * 1024 + 1024]
            xa = [xh, xh, xl][pi]
            last = None
            for ki in range(KD):
                lhs = xa[:, ki * P:(ki + 1) * P]
                for (ps, hb, lb) in projs:
                    wb = hb if pi != 1 else lb
                    for no in range(2):
                        last = te.matmul(
                            ps[:, no * 512:(no + 1) * 512], lhs,
                            ab[:, wb + ki * 1024 + no * 512: wb + ki * 1024 + (no + 1) * 512],
                            start=(pi == 0 and ki == 0), stop=(pi == 2 and ki == KD - 1))
            return last

        def attn_proj_mms(te, tau, projs):
            for pi in range(3):
                last = attn_proj_pass(te, tau, projs, pi)
            return last

        def transpose8(te, dst_psum, src_f32):
            last = None
            for ki in range(KD):
                last = te.transpose(dst_psum[:, ki * P:(ki + 1) * P],
                                    src_f32[:, ki * P:(ki + 1) * P], ident[:])
            return last

        def newton_rsqrt(t_in, sv, r0, rr, aa, bb, rs):
            vrecip(r0, sv)
            vtt(rr, r0, r0, OP.mult)
            vtt(aa, t_in, rr, OP.mult)
            vts(bb, aa, -0.5, 1.5, OP.mult, OP.add)
            return vtt(rs, r0, bb, OP.mult)

        tvc = lambda c: tv[:, c:c + 1]
        hzc = [0]

        def vrecip(out_ap, in_ap):
            hzc[0] += 1
            nc.vector.reciprocal(out_ap, in_ap).then_inc(S.sHZ, 1)
            nc.vector.wait_ge(S.sHZ, hzc[0])

        def _vhard(i, fence=None):
            if fence is None:
                hzc[0] += 1
                i.then_inc(S.sHZ, 1)
                nc.vector.wait_ge(S.sHZ, hzc[0])
            else:
                sem, val = fence
                i.then_inc(sem, 1)
                nc.vector.wait_ge(sem, val)
            return i

        def vts(out_ap, in0, s1, s2, op0, op1=None, fence=None):
            i = (nc.vector.tensor_scalar(out_ap, in0, s1, s2, op0, op1)
                 if op1 is not None else
                 nc.vector.tensor_scalar(out_ap, in0, s1, None, op0))
            return _vhard(i, fence)

        def vtt(out_ap, a_ap, b_ap, op, fence=None):
            i = nc.vector.tensor_tensor(out_ap, a_ap, b_ap, op)
            return _vhard(i, fence)

        def vred(out_ap, in_ap, ax, op, absval=None, fence=None):
            i = nc.vector.tensor_reduce(out_ap, in_ap, ax, op,
                                        apply_absolute_value=absval)
            return _vhard(i, fence)


        A_WQ_H, A_WQ_L, A_WK_H, A_WK_L = 0, 8192, 16384, 24576
        B_WV_H, B_WV_L, B_WG_H, B_WG_L, B_WO_H, B_WO_L = 0, 8192, 16384, 24576, 32768, 40960

        def afs(c0, n=1024):
            return af[:, c0:c0 + n]

        def wslot(i):
            k = i % 3
            if k < 2:
                return wfr[:, k * 1024:(k + 1) * 1024]
            return af[:, PH + 10240:PH + 11264]

        ffn_ntile = 96

        def ffn_tile_src(i):
            m, s = i // 32, i % 32
            if m < 2:
                return ffn_w_h[m][s * P:(s + 1) * P, :]
            r, q = s // 4, s % 4
            return wdown_h[r * P:(r + 1) * P, q * 1024:(q + 1) * 1024]

        def ffn_tile_dst(i):
            m, s = i // 32, i % 32
            if m < 2:
                return ffn_scr[m][s * P:(s + 1) * P, :]
            r, q = s // 4, s % 4
            return td_scr[r * P:(r + 1) * P, q * 1024:(q + 1) * 1024]

        block = e(nc.Block())

        @block.sync
        def _(sy):
            sy.dma_start(out=af[0:1, GB_BC:GB_BC + 1024], in_=gam_h[0:1, :]).then_inc(S.s_gb, 16)
            sy.dma_start(out=af[0:1, GB_BC + 1024:GB_BC + 2048], in_=bet_h[0:1, :]).then_inc(S.s_gb, 16)
            for i in range(16):
                wsrc = wq_h if i < 8 else wk_h
                ci = i % 8
                if i >= 2:
                    sy.wait_ge(S.sP_wsplit, i - 1)
                sy.dma_start(out=afs(PH + (i % 2) * 1024),
                             in_=wsrc[ci * P:(ci + 1) * P, :]).then_inc(S.sP_w, 16)
            for t in range(NT):
                if t >= 2:
                    sy.wait_ge(S.sA_tr, t - 1)
                else:
                    sy.wait_ge(S.sP_wsplit, 15 + t)
                sy.dma_start(out=afs(PH + (t % 2) * 1024),
                             in_=x_h[t * P:(t + 1) * P, :]).then_inc(S.sA_x, 16)
            sy.sem_inc(S.BAR, 1)
            sy.wait_ge(S.BAR, bar_val(0))
            if NPHASE >= 2:
                for i in range(24):
                    wsrc = [wv_h, wg_h, wo_h][i // 8]
                    ci = i % 8
                    if i >= 2:
                        sy.wait_ge(S.sB_wsplit, i - 1)
                    sy.dma_start(out=afs(PH + (i % 2) * 1024),
                                 in_=wsrc[ci * P:(ci + 1) * P, :]).then_inc(S.sB_w, 16)
                for t in range(NT):
                    if t >= 3:
                        sy.wait_ge(S.sB_y, t - 2)
                    elif t < 2:
                        sy.wait_ge(S.sB_wsplit, 23 + t)
                    sy.dma_start(out=afs(PH + (t % 3) * 1024),
                                 in_=x_h[t * P:(t + 1) * P, :]).then_inc(S.sB_x, 16)
                    if t >= 2:
                        sy.wait_ge(S.sB_P, t - 1)
                        sy.wait_ge(S.sB_zgtr, t - 1)
                    sy.dma_start(out=afs(PH + 3072 + (t % 2) * 1024),
                                 in_=q_scr[t * P:(t + 1) * P, :]).then_inc(S.sB_q, 16)
                if NPHASE >= 4:
                    sy.wait_ge(S.sB_vg, NT)
                    for ki in range(KD):
                        for ob in range(8):
                            sy.dma_start(
                                out=ab[:, ki * 4096 + ob * 512: ki * 4096 + (ob + 1) * 512],
                                in_=tg_scr[ob * 512:(ob + 1) * 512, ki * P:(ki + 1) * P],
                                transpose=True).then_inc(S.sC_tern, 16)
                    sy.wait_ge(S.sB_wo, NT)
                    sy.wait_ge(S.sB_zgsplit, NT)
                    for ki in range(KD):
                        for ob in range(8):
                            sy.dma_start(
                                out=ab[:, 32768 + ki * 4096 + ob * 512: 32768 + ki * 4096 + (ob + 1) * 512],
                                in_=tu_scr[ob * 512:(ob + 1) * 512, ki * P:(ki + 1) * P],
                                transpose=True).then_inc(S.sC_tern, 16)
                    sy.wait_ge(S.sB_y, NT)
                    for t in range(2):
                        sy.dma_start(out=afs(PH + (t % 2) * 1024),
                                     in_=y_scr[t * P:(t + 1) * P, :]).then_inc(S.sC_y, 16)
                sy.sem_inc(S.BAR, 1)
                sy.wait_ge(S.BAR, bar_val(1))
            if NPHASE >= 4:
                for t in range(2, NT):
                    if t >= 2:
                        sy.wait_ge(S.sC_intx, t - 1)
                    sy.dma_start(out=afs(PH + (t % 2) * 1024),
                                 in_=y_scr[t * P:(t + 1) * P, :]).then_inc(S.sC_y, 16)
                sy.sem_inc(S.BAR, 1)
                sy.wait_ge(S.BAR, bar_val(2))
                for ki in range(KF):
                    for no in range(2):
                        sy.dma_start(
                            out=ab[:, ki * 1024 + no * 512: ki * 1024 + (no + 1) * 512],
                            in_=td_scr[no * 512:(no + 1) * 512, ki * P:(ki + 1) * P],
                            transpose=True).then_inc(S.sD_tern, 16)
                for t in range(NT):
                    if t >= 2:
                        sy.wait_ge(S.sD_mm, t - 1)
                    sy.dma_start(out=af[:, PH + (t % 2) * 2048: PH + (t % 2) * 2048 + 2048].bitcast(BF16),
                                 in_=ih_scr[t * P:(t + 1) * P, :]).then_inc(S.sD_ih, 16)
                    if t >= 2:
                        sy.wait_ge(S.sD_y2, t - 1)
                    sy.dma_start(out=afs(PH + 4096 + (t % 2) * 1024),
                                 in_=y_scr[t * P:(t + 1) * P, :]).then_inc(S.sD_y, 16)
                sy.sem_inc(S.BAR, 1)
                sy.wait_ge(S.BAR, bar_val(3))

        @block.tensor
        def _(te):
            te.wait_ge(S.sG_init, 1)
            te.wait_ge(S.sP_wsplit, 16)
            te.wait_ge(S.sA_x, 16)
            transpose8(te, pD, afs(PH + 0)).then_inc(S.sA_tr, 1)
            for t in range(NT):
                te.wait_ge(S.sA_split, t + 1)
                te.wait_ge(S.sA_evac, t)
                attn_proj_pass(te, t, [(pA, A_WQ_H, A_WQ_L), (pB, A_WK_H, A_WK_L)], 0)
                attn_proj_pass(te, t, [(pA, A_WQ_H, A_WQ_L), (pB, A_WK_H, A_WK_L)], 1)
                if t + 1 < NT:
                    te.wait_ge(S.sA_x, 16 * (t + 2))
                    transpose8(te, pD, afs(PH + ((t + 1) % 2) * 1024)).then_inc(S.sA_tr, 1)
                if t >= 1:
                    sg = t - 1
                    te.wait_ge(S.sA_k1, sg + 1)
                    if sg == TPB:
                        te.wait_ge(S.sA_ksbccopy, 1)
                    k_sb = afs(PH + 4096 + (sg % 2) * 1024)
                    for no in range(2):
                        mm = te.matmul(pC[0:1, no * 512:(no + 1) * 512], ones[:],
                                       k_sb[:, no * 512:(no + 1) * 512],
                                       start=(sg % TPB == 0), stop=(sg % TPB == TPB - 1))
                    mm.then_inc(S.sA_ksmm, 1)
                    if sg % TPB == TPB - 1:
                        b = sg // TPB
                        te.wait_ge(S.sA_kscopy, b + 1)
                        for no in range(2):
                            mm = te.matmul(pC[:, no * 512:(no + 1) * 512], onescol[:],
                                           af[0:1, KSUM_BC + b * 1024 + no * 512: KSUM_BC + b * 1024 + (no + 1) * 512],
                                           start=True, stop=True)
                        mm.then_inc(S.sA_ksbc, 1)
                last = attn_proj_pass(te, t, [(pA, A_WQ_H, A_WQ_L), (pB, A_WK_H, A_WK_L)], 2)
                last.then_inc(S.sA_qk, 1)
            sg = NT - 1
            te.wait_ge(S.sA_k1, sg + 1)
            k_sb = afs(PH + 4096 + (sg % 2) * 1024)
            for no in range(2):
                mm = te.matmul(pC[0:1, no * 512:(no + 1) * 512], ones[:],
                               k_sb[:, no * 512:(no + 1) * 512],
                               start=(sg % TPB == 0), stop=True)
            mm.then_inc(S.sA_ksmm, 1)
            te.wait_ge(S.sA_kscopy, 2)
            for no in range(2):
                mm = te.matmul(pC[:, no * 512:(no + 1) * 512], onescol[:],
                               af[0:1, KSUM_BC + 1024 + no * 512: KSUM_BC + 1024 + (no + 1) * 512],
                               start=True, stop=True)
            mm.then_inc(S.sA_ksbc, 1)
            te.sem_inc(S.BAR, 1)
            te.wait_ge(S.BAR, bar_val(0))
            if NPHASE >= 2:
                te.wait_ge(S.sB_wsplit, 24)
                te.wait_ge(S.sB_x, 16)
                transpose8(te, pD, afs(PH + 0)).then_inc(S.sB_tr, 1)
                for t in range(NT + 1):
                    if t < NT:
                        te.wait_ge(S.sB_split, t + 1)
                        if t >= 1:
                            te.wait_ge(S.sB_z, t)
                            te.wait_ge(S.sB_sg, t)
                        attn_proj_pass(te, t, [(pA, B_WV_H, B_WV_L), (pB, B_WG_H, B_WG_L)], 0)
                        attn_proj_pass(te, t, [(pA, B_WV_H, B_WV_L), (pB, B_WG_H, B_WG_L)], 1)
                        if t + 1 < NT:
                            te.wait_ge(S.sB_x, 16 * (t + 2))
                            transpose8(te, pD, afs(PH + ((t + 1) % 3) * 1024)).then_inc(S.sB_tr, 1)
                        if t >= 1:
                            sg_ = t - 1
                            te.wait_ge(S.sB_zg, sg_ + 1)
                            te.wait_ge(S.sB_split, min(t + 2, NT))
                            transpose8(te, pD, afs(PH + 3072 + (sg_ % 2) * 1024)).then_inc(S.sB_zgtr, 1)
                        last = attn_proj_pass(te, t, [(pA, B_WV_H, B_WV_L), (pB, B_WG_H, B_WG_L)], 2)
                        last.then_inc(S.sB_vg, 1)
                    else:
                        sg_ = t - 1
                        te.wait_ge(S.sB_zg, sg_ + 1)
                        te.wait_ge(S.sB_split, NT)
                        transpose8(te, pD, afs(PH + 3072 + (sg_ % 2) * 1024)).then_inc(S.sB_zgtr, 1)
                    if t >= 1:
                        sg_ = t - 1
                        te.wait_ge(S.sB_zgsplit, sg_ + 1)
                        if sg_ >= 1:
                            te.wait_ge(S.sB_y, sg_)
                        lastw = None
                        for pi, zb in enumerate([ZG_H, ZG_H, ZG_L]):
                            wb = B_WO_H if pi != 1 else B_WO_L
                            for ki in range(KD):
                                lhs = ab[:, zb + ki * P: zb + (ki + 1) * P]
                                for no in range(2):
                                    lastw = te.matmul(
                                        pC[:, no * 512:(no + 1) * 512], lhs,
                                        ab[:, wb + ki * 1024 + no * 512: wb + ki * 1024 + (no + 1) * 512],
                                        start=(pi == 0 and ki == 0), stop=(pi == 2 and ki == KD - 1))
                        lastw.then_inc(S.sB_wo, 1)
                te.sem_inc(S.BAR, 1)
                te.wait_ge(S.BAR, bar_val(1))
            if NPHASE >= 4:
                te.wait_ge(S.sC_tern, 16 * 128)
                te.wait_ge(S.sC_intx, 1)
                transpose8(te, pC, afs(PH + 2048)).then_inc(S.sC_ixtr, 1)

                def ih_round(tt, r):
                    dst = pC if r % 2 == 0 else pD
                    if r == 0:
                        te.wait_ge(S.sC_inth, tt + 1)
                        if tt >= 1:
                            te.wait_ge(S.sC_ihT, 4 * (tt - 1) + 4)
                    elif r == 1 and tt >= 1:
                        te.wait_ge(S.sC_ihT, 4 * (tt - 1) + 4)
                    elif r >= 2:
                        te.wait_ge(S.sC_ihT, 4 * tt + r - 1)
                    lastr = None
                    for kk in range(8):
                        jc = r * 8 + kk
                        src = af[:, PERSIST + jc * P: PERSIST + (jc + 1) * P]
                        lastr = te.transpose(dst[:, kk * P:(kk + 1) * P], src, ident[:])
                    lastr.then_inc(S.sC_ihtr, 1)

                for t in range(NT + 1):
                    if t < NT:
                        te.wait_ge(S.sC_ixT, t + 1)
                        ixb = PH + 8192 + (t % 2) * 512
                        for oc in range(8):
                            if 8 * t + oc >= 2:
                                te.wait_ge(S.sC_sg, 8 * t + oc - 1)
                                te.wait_ge(S.sC_h, 8 * t + oc - 1)
                            po = (oc % 2) * 512
                            for ki in range(KD):
                                lhs = af[:, ixb + ki * 64: ixb + (ki + 1) * 64].bitcast(BF16)
                                g_rhs = ab[:, ki * 4096 + oc * 512: ki * 4096 + (oc + 1) * 512]
                                u_rhs = ab[:, 32768 + ki * 4096 + oc * 512: 32768 + ki * 4096 + (oc + 1) * 512]
                                te.matmul(pA[:, po:po + 512], lhs, g_rhs, start=(ki == 0), stop=(ki == KD - 1))
                                mm = te.matmul(pB[:, po:po + 512], lhs, u_rhs, start=(ki == 0), stop=(ki == KD - 1))
                            mm.then_inc(S.sC_gu, 1)
                            if t >= 1 and oc >= 4:
                                ih_round(t - 1, oc - 4)
                        if t + 1 < NT:
                            te.wait_ge(S.sC_intx, t + 2)
                            if t >= 1:
                                te.wait_ge(S.sC_ihT, 4 * (t - 1) + 3)
                            transpose8(te, pC, afs(PH + 2048)).then_inc(S.sC_ixtr, 1)
                    else:
                        for r in range(4):
                            ih_round(NT - 1, r)
                te.sem_inc(S.BAR, 1)
                te.wait_ge(S.BAR, bar_val(2))
                for t in range(NT):
                    te.wait_ge(S.sD_ih, 16 * (t + 1))
                    if t == 1:
                        te.wait_ge(S.sD_tern, 16 * 64)
                    if t >= 2:
                        te.wait_ge(S.sD_y2, t - 1)
                    ps = pA if t % 2 == 0 else pB
                    ihb = af[:, PH + (t % 2) * 2048: PH + (t % 2) * 2048 + 2048]
                    for ki in range(KF):
                        if t == 0:
                            te.wait_ge(S.sD_tern, 16 * 2 * (ki + 1))
                        lhs = ihb[:, ki * 64:(ki + 1) * 64].bitcast(BF16)
                        for no in range(2):
                            mm = te.matmul(ps[:, no * 512:(no + 1) * 512], lhs,
                                           ab[:, ki * 1024 + no * 512: ki * 1024 + (no + 1) * 512],
                                           start=(ki == 0), stop=(ki == KF - 1))
                    mm.then_inc(S.sD_mm, 1)
                te.sem_inc(S.BAR, 1)
                te.wait_ge(S.BAR, bar_val(3))

        @block.vector
        def _(vv):
            for i in range(16):
                vv.wait_ge(S.sP_w, 16 * (i + 1))
                base = [A_WQ_H, A_WK_H][i // 8]
                lbase = [A_WQ_L, A_WK_L][i // 8]
                ci = i % 8
                src = afs(PH + (i % 2) * 1024)
                hi = ab[:, base + ci * 1024: base + (ci + 1) * 1024]
                lo = ab[:, lbase + ci * 1024: lbase + (ci + 1) * 1024]
                vv.tensor_copy(hi, src)
                vv.tensor_tensor(lo, src, hi, OP.subtract).then_inc(S.sP_wsplit, 1)
            vv.wait_ge(S.sA_tr, 1)
            vv.tensor_copy(ab[:, XT_H:XT_H + 1024], pD)
            vv.tensor_tensor(ab[:, XT_L:XT_L + 1024], pD,
                             ab[:, XT_H:XT_H + 1024], OP.subtract).then_inc(S.sA_split, 1)
            for t in range(NT):
                par = t % 2
                if t == TPB:
                    vv.wait_ge(S.sA_ksmm, TPB)
                    vv.tensor_copy(af[0:1, KSUM_BC:KSUM_BC + 1024], pC[0:1, :]).then_inc(S.sA_kscopy, 1)
                    vv.wait_ge(S.sA_ksbc, 1)
                    vv.tensor_copy(af[:, KSUM_BC:KSUM_BC + 1024], pC).then_inc(S.sA_ksbccopy, 1)
                q_sb = afs(PH + 2048 + par * 1024)
                k_sb = afs(PH + 4096 + par * 1024)
                vv.wait_ge(S.sA_evac, t + 1)
                if t + 1 < NT:
                    vv.wait_ge(S.sA_tr, t + 2)
                    xh2 = ab[:, XT_H + ((t + 1) % 2) * 1024: XT_H + ((t + 1) % 2) * 1024 + 1024]
                    xl2 = ab[:, XT_L + ((t + 1) % 2) * 1024: XT_L + ((t + 1) % 2) * 1024 + 1024]
                    vv.tensor_copy(xh2, pD)
                    vv.tensor_tensor(xl2, pD, xh2, OP.subtract).then_inc(S.sA_split, 1)
                vv.tensor_scalar(afs(PH + 6144), q_sb, 0.0, None, OP.min).then_inc(S.sA_minq, 1)
                vv.tensor_scalar(afs(PH + 8192), k_sb, 0.0, None, OP.min).then_inc(S.sA_mink, 1)
                vv.wait_ge(S.sA_expq, t + 1)
                vv.scalar_tensor_tensor(q_sb, q_sb, 0.0, afs(PH + 7168), OP.max, OP.add).then_inc(S.sA_q1, 1)
                vv.wait_ge(S.sA_expk, t + 1)
                vv.scalar_tensor_tensor(k_sb, k_sb, 0.0, afs(PH + 9216), OP.max, OP.add).then_inc(S.sA_k1, 1)
                vv.tensor_tensor(afs(PH + 6144), q_sb, k_sb, OP.mult)
                vv.tensor_reduce(af[:, QK_ALL + t * 16: QK_ALL + (t + 1) * 16],
                                 afs(PH + 6144).rearrange("p (h d) -> p h d", h=NH),
                                 AX.X, OP.add)
                if t == 0:
                    vv.memset(racc128[:], 0.0)
                if t < 32:
                    for j in range(3 * t, 3 * t + 3):
                        m = j // 32
                        vv.wait_ge(S.sF_dma, 16 * (j + 1))
                        vred(racc128[:, 3:4], wslot(j),
                             AX.X, OP.add, absval=True)
                        vtt(racc128[:, m:m + 1], racc128[:, m:m + 1],
                            racc128[:, 3:4], OP.add, fence=(S.sF_vr, j + 1))
                    if t == 31:
                        vv.sem_inc(S.sF_sum, 1)
                if t == 33:
                    vv.wait_ge(S.sF_gm, 3)
                    for m in range(3):
                        src = racc[0:1, m:m + 1]
                        vts(swu[0:1, m:m + 1], src, RMEAN_W, 1e-5, OP.mult, OP.max)
                        vrecip(swu[0:1, m:m + 1], swu[0:1, m:m + 1])
                        vrecip(swu[0:1, 4 + m:5 + m], swu[0:1, m:m + 1])
                        vv.engine_nop().then_inc(S.sF_scal, 1)
                if t >= 34:
                    for i in range(4 * (t - 34), min(ffn_ntile, 4 * (t - 33))):
                        vv.wait_ge(S.sF_s1, i + 1)
                        if i >= 2:
                            vv.wait_ge(S.st_tern, 16 * (i - 1))
                        ws = wslot(i)
                        vv.tensor_scalar(ws, ws, CM, 1.0, OP.subtract, OP.min)
                        vv.tensor_scalar(tfr[:, (i % 2) * 1024:(i % 2 + 1) * 1024],
                                         ws, -1.0, None, OP.max).then_inc(S.sF_vq, 1)
            vv.wait_ge(S.sA_ksmm, NT)
            vv.tensor_copy(af[0:1, KSUM_BC + 1024:KSUM_BC + 2048], pC[0:1, :]).then_inc(S.sA_kscopy, 1)
            vv.wait_ge(S.sA_ksbc, 2)
            vv.tensor_copy(af[:, KSUM_BC + 1024:KSUM_BC + 2048], pC).then_inc(S.sA_ksbccopy, 1)
            vv.sem_inc(S.BAR, 1)
            vv.wait_ge(S.BAR, bar_val(0))
            if NPHASE >= 2:
                for i in range(24):
                    vv.wait_ge(S.sB_w, 16 * (i + 1))
                    base = [B_WV_H, B_WG_H, B_WO_H][i // 8]
                    lbase = [B_WV_L, B_WG_L, B_WO_L][i // 8]
                    ci = i % 8
                    src = afs(PH + (i % 2) * 1024)
                    vv.tensor_copy(ab[:, base + ci * 1024: base + (ci + 1) * 1024], src)
                    vv.tensor_tensor(ab[:, lbase + ci * 1024: lbase + (ci + 1) * 1024], src,
                                     ab[:, base + ci * 1024: base + (ci + 1) * 1024],
                                     OP.subtract).then_inc(S.sB_wsplit, 1)
                vv.wait_ge(S.sF_bc, 224)
                vv.wait_ge(S.sB_tr, 1)
                xh = ab[:, XT_H:XT_H + 1024]
                xl = ab[:, XT_L:XT_L + 1024]
                vv.tensor_copy(xh, pD)
                vv.tensor_tensor(xl, pD, xh, OP.subtract).then_inc(S.sB_split, 1)
                for t in range(NT + 1):
                    if t < NT:
                        par = t % 2
                        b = t // TPB
                        vv.wait_ge(S.sB_q, 16 * (t + 1))
                        qin = afs(PH + 3072 + par * 1024)
                        tmpP = afs(PH + 5120)
                        vv.tensor_tensor(tmpP, qin, af[:, KSUM_BC + b * 1024: KSUM_BC + (b + 1) * 1024],
                                         OP.mult).then_inc(S.sB_P, 1)
                        vred(tv[:, NRM16:NRM16 + 16],
                             tmpP.rearrange("p (h d) -> p h d", h=NH), AX.X, OP.add)
                        vts(tv[:, NRM16:NRM16 + 16], tv[:, NRM16:NRM16 + 16],
                            1e-6, None, OP.add)
                        vrecip(tv[:, REC16:REC16 + 16], tv[:, NRM16:NRM16 + 16])
                        vtt(tv[:, R16:R16 + 16], af[:, QK_ALL + t * 16: QK_ALL + (t + 1) * 16],
                            tv[:, REC16:REC16 + 16], OP.mult)
                        if t + 1 < NT:
                            vv.wait_ge(S.sB_tr, t + 2)
                            vv.wait_ge(S.sB_vg, t)
                            xh = ab[:, XT_H + ((t + 1) % 2) * 1024: XT_H + ((t + 1) % 2) * 1024 + 1024]
                            xl = ab[:, XT_L + ((t + 1) % 2) * 1024: XT_L + ((t + 1) % 2) * 1024 + 1024]
                            vv.tensor_copy(xh, pD)
                            vv.tensor_tensor(xl, pD, xh, OP.subtract).then_inc(S.sB_split, 1)
                        vv.wait_ge(S.sB_vg, t + 1)
                        z = afs(PH + 6144)
                        rb = tv[:, R16:R16 + 16].unsqueeze(2).broadcast_to([P, NH, DH])
                        vv.scalar_tensor_tensor(z.rearrange("p (h d) -> p h d", h=NH),
                                                pA.rearrange("p (h d) -> p h d", h=NH),
                                                1.0, rb, OP.mult, OP.mult,
                                                accum_out=tvc(ZSUM)).then_inc(S.sB_z, 1)
                        vv.wait_ge(S.sB_z, t + 1)
                        if t >= 1:
                            sg_e = t - 1
                            vv.wait_ge(S.sB_zgtr, sg_e + 1)
                            if sg_e >= 1:
                                vv.wait_ge(S.sB_wo, sg_e)
                            vv.tensor_copy(ab[:, ZG_H:ZG_H + 1024], pD)
                            vv.tensor_tensor(ab[:, ZG_L:ZG_L + 1024], pD, ab[:, ZG_H:ZG_H + 1024],
                                             OP.subtract).then_inc(S.sB_zgsplit, 1)
                        vts(tvc(MUz), tvc(ZSUM), R1024, None, OP.mult)
                        vv.wait_ge(S.sB_sqz, t + 1)
                        vts(tvc(AAz), tvc(MUz), tvc(MUz), 1e-5, OP.mult, OP.subtract)
                        vts(tvc(VVz), tvc(ZSS), R1024, tvc(AAz), OP.mult, OP.subtract,
                            fence=(S.sB_varz, t + 1))
                        vv.wait_ge(S.sB_sqrtz, t + 1)
                        newton_rsqrt(tvc(VVz), tvc(SVz), tvc(R0z), tvc(RRz), tvc(AAz), tvc(BBz), tvc(RSz))
                        vv.tensor_scalar(tmpP, z, tvc(MUz), tvc(RSz), OP.subtract, OP.mult)
                        vv.tensor_tensor(tmpP, tmpP, af[:, GB_BC:GB_BC + 1024], OP.mult)
                        vv.tensor_tensor(tmpP, tmpP, af[:, GB_BC + 1024:GB_BC + 2048], OP.add)
                        vv.wait_ge(S.sB_sg, t + 1)
                        vv.tensor_tensor(qin, tmpP, afs(PH + 8192), OP.mult).then_inc(S.sB_zg, 1)
                    if t == NT:
                        sg_e = t - 1
                        vv.wait_ge(S.sB_zgtr, sg_e + 1)
                        vv.wait_ge(S.sB_wo, sg_e)
                        vv.tensor_copy(ab[:, ZG_H:ZG_H + 1024], pD)
                        vv.tensor_tensor(ab[:, ZG_L:ZG_L + 1024], pD, ab[:, ZG_H:ZG_H + 1024],
                                         OP.subtract).then_inc(S.sB_zgsplit, 1)
                    if t >= 1:
                        sg_ = t - 1
                        tmp1 = afs(PH + 7168)
                        vv.wait_ge(S.sB_wo, sg_ + 1)
                        vv.scalar_tensor_tensor(tmp1, afs(PH + (sg_ % 3) * 1024), 1.0, pC,
                                                OP.mult, OP.add, accum_out=tvc(YSUM)).then_inc(S.sB_y, 1)
                        vv.wait_ge(S.sB_y, sg_ + 1)
                        vts(tvc(MUy), tvc(YSUM), R1024, None, OP.mult)
                        vv.wait_ge(S.sB_sqy, sg_ + 1)
                        vts(tvc(AAy), tvc(MUy), tvc(MUy), 1e-5, OP.mult, OP.subtract)
                        vts(tvc(VVy), tvc(YSS), R1024, tvc(AAy), OP.mult, OP.subtract,
                            fence=(S.sB_vary, sg_ + 1))
                        vv.wait_ge(S.sB_sqrty, sg_ + 1)
                        newton_rsqrt(tvc(VVy), tvc(SVy), tvc(R0y), tvc(RRy), tvc(AAy), tvc(BBy), tvc(RSy))
                        if sg_ >= 2:
                            vv.wait_ge(S.st_y, 16 * (sg_ - 1))
                        vv.tensor_scalar(afs(PH + 9216 + (sg_ % 2) * 1024), tmp1, tvc(MUy), tvc(RSy),
                                         OP.subtract, OP.mult).then_inc(S.sB_y1, 1)
                vv.sem_inc(S.BAR, 1)
                vv.wait_ge(S.BAR, bar_val(1))
            if NPHASE >= 4:
                def quantA(tq, part):
                    par = tq % 2
                    yb = afs(PH + par * 1024)
                    if part == 0:
                        vv.wait_ge(S.sC_y, 16 * (tq + 1))
                        vred(tvc(MX), yb, AX.X, OP.max, absval=True)
                        vts(tvc(MC), tvc(MX), 1e-5, None, OP.max)
                    else:
                        vrecip(tvc(RECM), tvc(MC))
                        vts(tvc(SXV + 26 * par), tvc(RECM), 127.0, None, OP.mult)

                def quantB(tq, part):
                    par = tq % 2
                    yb = afs(PH + par * 1024)
                    intx = afs(PH + 2048)
                    if part == 0:
                        if tq >= 2:
                            vv.wait_ge(S.sC_sg, 8 * (tq - 2) + 8)
                        vrecip(tvc(RECM), tvc(SXV + 26 * par))
                        vtt(tvc(CGc + 22 * par), tvc(RECM), uwbc[:, 0:1], OP.mult)
                        vtt(tvc(CUc + 22 * par), tvc(RECM), uwbc[:, 1:2], OP.mult)
                    elif part == 1:
                        if tq >= 1:
                            vv.wait_ge(S.sC_ixtr, tq)
                        vv.tensor_scalar(intx, yb, tvc(SXV + 26 * par), CM, OP.mult, OP.add)
                    elif part == 2:
                        vv.tensor_scalar(intx, intx, CM, 127.0, OP.subtract, OP.min)
                    else:
                        vv.tensor_scalar(intx, intx, -128.0, None, OP.max).then_inc(S.sC_intx, 1)

                def y_quant(tq):
                    quantA(tq, 0)
                    quantA(tq, 1)
                    for pp in range(4):
                        quantB(tq, pp)

                y_quant(0)
                for t in range(NT + 1):
                    if dbg == "c1" and t >= 1:
                        vv.wait_ge(S.sDbg, 16 + 64 * t)
                    if t < NT:
                        if t == 0:
                            y_quant(1)
                        for oc in range(8):
                            po = (oc % 2) * 512
                            vv.wait_ge(S.sC_sg, 8 * t + oc + 1)
                            h_oc = af[:, PH + 3072 + oc * 512: PH + 3072 + (oc + 1) * 512]
                            vv.scalar_tensor_tensor(h_oc, pB[:, po:po + 512], tvc(CUc + 22 * (t % 2)),
                                                    afs(PH + 7168 + (oc % 2) * 512, 512),
                                                    OP.mult, OP.mult).then_inc(S.sC_h, 1)
                            vv.tensor_reduce(tv[:, HM8 + oc:HM8 + oc + 1], h_oc, AX.X,
                                             OP.max, apply_absolute_value=True)
                            if t >= 2 and oc < 4:
                                quantB(t + 1, oc)
                            if t >= 1 and 4 <= oc < 6 and t + 2 < NT:
                                quantA(t + 2, oc - 4)
                            if t >= 1 and oc >= 4:
                                r = oc - 4
                                vv.wait_ge(S.sC_ihtr, 4 * (t - 1) + r + 1)
                                if r == 0 and t >= 2:
                                    vv.wait_ge(S.st_ih, 16 * (t - 1))
                                src = pC if r % 2 == 0 else pD
                                vv.tensor_copy(af[:, PH + 9216 + r * 512: PH + 9216 + (r + 1) * 512].bitcast(BF16),
                                               src).then_inc(S.sC_ihT, 1)
                        vv.engine_nop()
                        vred(tvc(HM), tv[:, HM8:HM8 + 8], AX.X, OP.max)
                        vts(tvc(MH), tvc(HM), 1e-5, None, OP.max)
                        vrecip(tvc(RH), tvc(MH))
                        vts(tvc(SH), tvc(RH), 127.0, None, OP.mult)
                        vrecip(tvc(REC3), tvc(SH))
                        vtt(cd_all[:, t:t + 1], tvc(REC3), uwbc[:, 2:3], OP.mult)
                        if t >= 1:
                            vv.wait_ge(S.sC_ihtr, 4 * t)
                        vv.engine_nop().then_inc(S.sC_sh, 1)
                        ih = af[:, PERSIST:PERSIST + 4096]
                        vv.wait_ge(S.sC_ih1, t + 1)
                        vv.tensor_scalar(ih, ih, CM, 127.0, OP.subtract, OP.min)
                        vv.tensor_scalar(ih, ih, -128.0, None, OP.max).then_inc(S.sC_inth, 1)
                        if t == 0 and NT > 2:
                            y_quant(2)
                    else:
                        for r in range(4):
                            vv.wait_ge(S.sC_ihtr, 4 * (NT - 1) + r + 1)
                            if r == 0:
                                vv.wait_ge(S.st_ih, 16 * (NT - 1))
                            src = pC if r % 2 == 0 else pD
                            vv.tensor_copy(af[:, PH + 9216 + r * 512: PH + 9216 + (r + 1) * 512].bitcast(BF16),
                                           src).then_inc(S.sC_ihT, 1)
                vv.sem_inc(S.BAR, 1)
                vv.wait_ge(S.BAR, bar_val(2))
                for t in range(NT):
                    par = t % 2
                    ps = pA if par == 0 else pB
                    vv.wait_ge(S.sD_mm, t + 1)
                    vv.wait_ge(S.sD_y, 16 * (t + 1))
                    y2 = afs(PH + 6144)
                    vv.scalar_tensor_tensor(y2, ps, cd_all[:, t:t + 1],
                                            afs(PH + 4096 + par * 1024),
                                            OP.mult, OP.add, accum_out=tvc(YSUM)).then_inc(S.sD_y2, 1)
                    vv.wait_ge(S.sD_y2, t + 1)
                    vts(tvc(MUy), tvc(YSUM), R1024, None, OP.mult)
                    vv.wait_ge(S.sD_sq, t + 1)
                    vts(tvc(VVy), tvc(YSS), R1024, None, OP.mult)
                    vtt(tvc(AAy), tvc(MUy), tvc(MUy), OP.mult)
                    vtt(tvc(VVy), tvc(VVy), tvc(AAy), OP.subtract)
                    vts(tvc(VVy), tvc(VVy), 1e-5, None, OP.add, fence=(S.sD_var, t + 1))
                    vv.wait_ge(S.sD_sqrt, t + 1)
                    newton_rsqrt(tvc(VVy), tvc(SVy), tvc(R0y), tvc(RRy), tvc(AAy), tvc(BBy), tvc(RSy))
                    if t >= 2:
                        vv.wait_ge(S.st_out, 16 * (t - 1))
                    vv.tensor_scalar(afs(PH + 7168 + par * 1024), y2, tvc(MUy), tvc(RSy),
                                     OP.subtract, OP.mult).then_inc(S.sD_out, 1)
                vv.sem_inc(S.BAR, 1)
                vv.wait_ge(S.BAR, bar_val(3))

        @block.scalar
        def _(sc):
            for t in range(NT):
                par = t % 2
                sc.wait_ge(S.sA_qk, t + 1)
                if t >= 2:
                    sc.wait_ge(S.st_q, 16 * (t - 1))
                    sc.wait_ge(S.sA_ksmm, t - 1)
                sc.activation(afs(PH + 2048 + par * 1024), pA, AF.Identity)
                sc.activation(afs(PH + 4096 + par * 1024), pB, AF.Identity).then_inc(S.sA_evac, 1)
                sc.wait_ge(S.sA_minq, t + 1)
                sc.activation(afs(PH + 7168), afs(PH + 6144), AF.Exp).then_inc(S.sA_expq, 1)
                sc.wait_ge(S.sA_mink, t + 1)
                sc.activation(afs(PH + 9216), afs(PH + 8192), AF.Exp).then_inc(S.sA_expk, 1)
                sc.wait_ge(S.sA_q1, t + 1)
                sc.dma_start(out=q_scr[t * P:(t + 1) * P, :],
                             in_=afs(PH + 2048 + par * 1024)).then_inc(S.st_q, 16)
                if t < 32:
                    for j in range(3 * t, 3 * t + 3):
                        if j >= 3:
                            sc.wait_ge(S.sF_vr, j - 2)
                        sc.dma_start(out=wslot(j),
                                     in_=ffn_tile_src(j)).then_inc(S.sF_dma, 16)
                if t >= 34:
                    for i in range(4 * (t - 34), min(ffn_ntile, 4 * (t - 33))):
                        m = i // 32
                        if i % 32 == 0:
                            sc.wait_ge(S.sF_bc, 224 + 256 * (m + 1))
                        sc.wait_ge(S.sF_qdma, 16 * (i + 1))
                        sc.activation(wslot(i),
                                      wslot(i),
                                      AF.Identity, scale=swbc[:, m:m + 1],
                                      bias=cmb[:]).then_inc(S.sF_s1, 1)
            sc.wait_ge(S.st_q, 16 * NT)
            sc.sem_inc(S.BAR, 1)
            sc.wait_ge(S.BAR, bar_val(0))
            if NPHASE >= 2:
                for t in range(NT + 1):
                    if t < NT:
                        sc.wait_ge(S.sB_z, t + 1)
                        sc.activation(afs(PH + 8192), afs(PH + 6144), AF.Square,
                                      accum_out=tvc(ZSS)).then_inc(S.sB_sqz, 1)
                        sc.wait_ge(S.sB_varz, t + 1)
                        sc.activation(tvc(SVz), tvc(VVz), AF.Sqrt).then_inc(S.sB_sqrtz, 1)
                        sc.wait_ge(S.sB_vg, t + 1)
                        if t >= 1:
                            sc.wait_ge(S.sB_zg, t)
                        sc.activation(afs(PH + 8192), pB, AF.Silu).then_inc(S.sB_sg, 1)
                    if t >= 1:
                        sg_ = t - 1
                        sc.wait_ge(S.sB_y, sg_ + 1)
                        sc.activation(afs(PH + 6144), afs(PH + 7168), AF.Square,
                                      accum_out=tvc(YSS)).then_inc(S.sB_sqy, 1)
                        sc.wait_ge(S.sB_vary, sg_ + 1)
                        sc.activation(tvc(SVy), tvc(VVy), AF.Sqrt).then_inc(S.sB_sqrty, 1)
                        sc.wait_ge(S.sB_y1, sg_ + 1)
                        sc.dma_start(out=y_scr[sg_ * P:(sg_ + 1) * P, :],
                                     in_=afs(PH + 9216 + (sg_ % 2) * 1024)).then_inc(S.st_y, 16)
                sc.wait_ge(S.st_y, 16 * NT)
                sc.sem_inc(S.BAR, 1)
                sc.wait_ge(S.BAR, bar_val(1))
            if NPHASE >= 4:
                sc.wait_ge(S.sC_ixtr, 1)
                sc.activation(af[:, PH + 8192:PH + 8704].bitcast(BF16), pC,
                              AF.Identity).then_inc(S.sC_ixT, 1)
                for t in range(NT + 1):
                    if t < NT:
                        for oc in range(8):
                            po = (oc % 2) * 512
                            sc.wait_ge(S.sC_gu, 8 * t + oc + 1)
                            if 8 * t + oc >= 2:
                                sc.wait_ge(S.sC_h, 8 * t + oc - 1)
                            sc.activation(afs(PH + 7168 + (oc % 2) * 512, 512),
                                          pA[:, po:po + 512], AF.Silu,
                                          scale=tvc(CGc + 22 * (t % 2))).then_inc(S.sC_sg, 1)
                        if t + 1 < NT:
                            sc.wait_ge(S.sC_ixtr, t + 2)
                            dst = PH + 8192 + ((t + 1) % 2) * 512
                            sc.activation(af[:, dst:dst + 512].bitcast(BF16), pC,
                                          AF.Identity).then_inc(S.sC_ixT, 1)
                        sc.wait_ge(S.sC_sh, t + 1)
                        sc.activation(af[:, PERSIST:PERSIST + 4096],
                                      af[:, PH + 3072:PH + 7168], AF.Identity,
                                      scale=tvc(SH), bias=cmb[:]).then_inc(S.sC_ih1, 1)
                    if t >= 1:
                        sg_ = t - 1
                        sc.wait_ge(S.sC_ihT, 4 * (sg_ + 1))
                        sc.dma_start(out=ih_scr[sg_ * P:(sg_ + 1) * P, :],
                                     in_=af[:, PH + 9216:PH + 11264].bitcast(BF16)).then_inc(S.st_ih, 16)
                sc.wait_ge(S.st_ih, 16 * NT)
                sc.sem_inc(S.BAR, 1)
                sc.wait_ge(S.BAR, bar_val(2))
                for t in range(NT):
                    par = t % 2
                    sc.wait_ge(S.sD_y2, t + 1)
                    sc.activation(afs(PH + 9216), afs(PH + 6144), AF.Square,
                                  accum_out=tvc(YSS)).then_inc(S.sD_sq, 1)
                    sc.wait_ge(S.sD_var, t + 1)
                    sc.activation(tvc(SVy), tvc(VVy), AF.Sqrt).then_inc(S.sD_sqrt, 1)
                    sc.wait_ge(S.sD_out, t + 1)
                    sc.dma_start(out=out_h[t * P:(t + 1) * P, :],
                                 in_=afs(PH + 7168 + par * 1024)).then_inc(S.st_out, 16)
                sc.wait_ge(S.st_out, 16 * NT)
                sc.sem_inc(S.BAR, 1)
                sc.wait_ge(S.BAR, bar_val(3))

        @block.gpsimd
        def _(gp):
            gp.memset(ident[:], 0.0)
            gp.affine_select(out=ident[:], in_=ident[:], compare_op=OP.not_equal,
                             fill=1.0, base=0, pattern=[[-1, P]], channel_multiplier=1)
            gp.memset(ones[:], 1.0)
            gp.memset(onescol[:], 1.0)
            gp.memset(racc[:], 0.0)
            gp.memset(cmb[:], CM)
            gp.sem_inc(S.sG_init, 1)
            fbc = [0]

            def pdouble(tens, c0, c1):
                k = 1
                while k < P:
                    n = min(k, P - k)
                    gp.dma_start(out=tens[k:k + n, c0:c1],
                                 in_=tens[0:n, c0:c1]).then_inc(S.sF_bc, 16)
                    fbc[0] += 16
                    gp.wait_ge(S.sF_bc, fbc[0])
                    k *= 2

            gp.wait_ge(S.s_gb, 32)
            pdouble(af, GB_BC, GB_BC + 1024)
            pdouble(af, GB_BC + 1024, GB_BC + 2048)
            gp.wait_ge(S.sF_sum, 1)
            for m in range(3):
                gp.tensor_reduce(racc[0:1, m:m + 1], racc128[:, m:m + 1],
                                 AX.C, OP.add).then_inc(S.sF_gm, 1)
            for m in range(3):
                gp.wait_ge(S.sF_scal, m + 1)
                gp.dma_start(out=swbc[0:1, m:m + 1],
                             in_=swu[0:1, m:m + 1]).then_inc(S.sF_bc, 16)
                fbc[0] += 16
                gp.wait_ge(S.sF_bc, fbc[0])
                pdouble(swbc[:], m, m + 1)
                gp.dma_start(out=uwbc[0:1, m:m + 1],
                             in_=swu[0:1, 4 + m:5 + m]).then_inc(S.sF_bc, 16)
                fbc[0] += 16
                gp.wait_ge(S.sF_bc, fbc[0])
                pdouble(uwbc[:], m, m + 1)
            for i0 in range(3):
                gp.dma_start(out=wslot(i0),
                             in_=ffn_tile_src(i0)).then_inc(S.sF_qdma, 16)
            for i in range(ffn_ntile):
                gp.wait_ge(S.sF_vq, i + 1)
                gp.dma_start(out=ffn_tile_dst(i),
                             in_=tfr[:, (i % 2) * 1024:(i % 2 + 1) * 1024]).then_inc(S.st_tern, 16)
                if i + 3 < ffn_ntile:
                    gp.dma_start(out=wslot(i + 3),
                                 in_=ffn_tile_src(i + 3)).then_inc(S.sF_qdma, 16)
            gp.wait_ge(S.st_tern, 16 * ffn_ntile)
            gp.sem_inc(S.BAR, 1)
            gp.wait_ge(S.BAR, bar_val(0))
            if NPHASE >= 2:
                gp.sem_inc(S.BAR, 1)
                gp.wait_ge(S.BAR, bar_val(1))
            if NPHASE >= 4:
                if dbg == "c1":
                    gp.wait_ge(S.sC_tern, 16 * 128)
                    gp.dma_start(out=dbg2_h[:, :], in_=ab[:, :]).then_inc(S.sDbg, 16)
                    gp.wait_ge(S.sDbg, 16)
                    for t in range(NT):
                        gp.wait_ge(S.sC_inth, t + 1)
                        r0, r1 = t * P, (t + 1) * P
                        gp.dma_start(out=dbg_h[r0:r1, 0:1024], in_=af[:, PH + 2048:PH + 3072]).then_inc(S.sDbg, 16)
                        gp.dma_start(out=dbg_h[r0:r1, 1024:5120], in_=af[:, PH + 3072:PH + 7168]).then_inc(S.sDbg, 16)
                        gp.dma_start(out=dbg_h[r0:r1, 5120:9216], in_=af[:, PERSIST:PERSIST + 4096]).then_inc(S.sDbg, 16)
                        gp.dma_start(out=dbg_h[r0:r1, 9216:9312], in_=tv[:]).then_inc(S.sDbg, 16)
                    gp.wait_ge(S.sDbg, 16 + 64 * NT)
                gp.sem_inc(S.BAR, 1)
                gp.wait_ge(S.BAR, bar_val(2))
                gp.sem_inc(S.BAR, 1)
                gp.wait_ge(S.BAR, bar_val(3))
            if stage == 1:
                for t in range(NT):
                    gp.dma_start(out=out_h[t * P:(t + 1) * P, :],
                                 in_=q_scr[t * P:(t + 1) * P, :]).then_inc(S.st_out, 16)
                gp.wait_ge(S.st_out, 16 * NT)
            elif stage == 2:
                for t in range(NT):
                    gp.dma_start(out=out_h[t * P:(t + 1) * P, :],
                                 in_=y_scr[t * P:(t + 1) * P, :]).then_inc(S.st_out, 16)
                gp.wait_ge(S.st_out, 16 * NT)

    return nc


def kernel(**inputs):
    x = np.ascontiguousarray(np.asarray(inputs["L0_attn_hidden_states"], dtype=np.float32))
    B, Sq, Dm = x.shape
    assert (B, Sq, Dm) == (16, 4096, 1024)
    w = {k: np.ascontiguousarray(np.asarray(inputs[k], dtype=np.float32))
         for k in ["L0_attn_Wq", "L0_attn_Wk", "L0_attn_Wv", "L0_attn_Wg", "L0_attn_Wo",
                   "L0_ffn_w_gate", "L0_ffn_w_up", "L0_ffn_w_down"]}
    gam = np.ascontiguousarray(np.asarray(inputs["L0_attn_gamma"], dtype=np.float32).reshape(1, Dm))
    bet = np.ascontiguousarray(np.asarray(inputs["L0_attn_beta"], dtype=np.float32).reshape(1, Dm))
    nc = build(stage=3)
    in_maps = []
    for c in range(8):
        m = dict(w)
        m["L0_attn_gamma"] = gam
        m["L0_attn_beta"] = bet
        m["L0_attn_hidden_states"] = x[2 * c:2 * c + 2].reshape(2 * Sq, Dm)
        in_maps.append(m)
    res = run_bass_kernel_spmd(nc, in_maps, core_ids=list(range(8)))
    out = np.concatenate([res.results[c]["out"].reshape(2, Sq, Dm) for c in range(8)], axis=0)
    return out
